# version 07
# baseline (speedup 1.0000x reference)
import numpy as np
import ml_dtypes
from contextlib import ExitStack

import concourse.bass as bass
import concourse.mybir as mybir
from concourse.bass_utils import run_bass_kernel_spmd

F32 = mybir.dt.float32
BF16 = mybir.dt.bfloat16
AX = mybir.AxisListType.X
AF = mybir.ActivationFunctionType
ALU = mybir.AluOpType

NH, DN, DR, DV = 16, 128, 64, 128
QR, KVR, D = 1536, 512, 2048
S = 4096
TOK = 1024
EPS_RMS, EPS_LN = 1e-6, 1e-5
SCALE = float(1.0 / np.sqrt(DN + DR))
PI = float(np.pi)
LN1E4 = float(np.log(10000.0))
NDC = D // 128
NRC = QR // 128
NKC = KVR // 128
POFF = [512 * t * (t - 1) // 2 for t in range(1, 10)]


class Sem:
    __slots__ = ("h", "n")

    def __init__(self, nc, st, name):
        self.h = st.enter_context(nc.semaphore(name))
        self.n = 0

    def inc(self, inst, k=1):
        inst.then_inc(self.h, k)
        self.n += k

    def wait(self, eng):
        if self.n:
            eng.wait_ge(self.h, self.n)


def build_nc(debug=False):
    nc = bass.Bass("TRN2", target_bir_lowering=False)
    ten, vec, sca, gps, syn = nc.tensor, nc.vector, nc.scalar, nc.gpsimd, nc.sync

    def din(name, shape):
        return nc.dram_tensor(name, shape, F32, kind="ExternalInput").ap()

    xo = din("xo", [D, TOK])
    xf = din("xf", [D, S])
    wqd = din("wqd", [D, QR])
    wqu = din("wqu", [QR, D])
    wqr = din("wqr", [QR, NH * DR])
    qnw = din("qnw", [128, NRC])
    wkd = din("wkd", [D, KVR])
    knw = din("knw", [128, NKC])
    wku = din("wku", [KVR, NH * 256])
    wkr = din("wkr", [D, DR])
    wo_ = din("wo", [D, D])
    rw = din("rw", [D, 8])
    sg = din("sg", [D, 8192])
    su = din("su", [D, 8192])
    sd = din("sd", [8192, D])
    rg = din("rg", [D, 1024])
    ru = din("ru", [D, 1024])
    rd = din("rd", [1024, D])
    msk = din("msk", [128, 512])
    pos = din("pos", [1, TOK])
    out_ = nc.dram_tensor("out", [D, TOK], F32, kind="ExternalOutput").ap()
    qspn = nc.dram_tensor("qspn", [NH, 128, TOK], BF16, kind="Internal").ap()
    qspp = nc.dram_tensor("qspp", [NH, DR, TOK], BF16, kind="Internal").ap()
    osp = nc.dram_tensor("osp", [128, NH * TOK], BF16, kind="Internal").ap()

    dbg = {}
    if debug:
        for nm, shp in [("d_qc", [128, NRC * TOK]), ("d_kvc", [128, NKC * S]),
                        ("d_kr", [DR, S]), ("d_obf", [128, NH * TOK]),
                        ("d_x1", [128, NDC * TOK])]:
            dbg[nm] = nc.dram_tensor(nm, shp, F32, kind="ExternalOutput").ap()

    st = ExitStack()
    with st:
        def sem(name):
            return Sem(nc, st, name)

        def sb(name, shape, dt):
            return st.enter_context(nc.sbuf_tensor(name, shape, dt))

        def ps(name, shape):
            return st.enter_context(nc.psum_tensor(name, shape, F32))

        ident_f = sb("ident_f", [128, 128], F32)
        iota_t = sb("iota_t", [128, 128], F32)
        ones_r = sb("ones_r", [1, 128], F32)
        ones_br = sb("ones_br", [1, 128], BF16)
        ones_c = sb("ones_c", [128, 1], F32)
        masksb = sb("masksb", [128, 512], F32)
        ifq = sb("ifq", [32, 1], F32)
        qnw_sb = sb("qnw_sb", [128, NRC], F32)
        knw_sb = sb("knw_sb", [128, NKC], F32)
        negpi = sb("negpi", [128, 1], F32)
        bar_s = sb("bar_s", [1, 4], F32)

        s_init = sem("s_init")
        s_hw = sem("s_hw")
        s_bar = sem("s_bar")
        s_spill = sem("s_spill")

        def barrier():
            vec.tensor_copy(bar_s[:, 0:1], bar_s[:, 1:2]).then_inc(s_bar.h, 1)
            sca.activation(bar_s[:, 2:3], bar_s[:, 1:2],
                           AF.Copy).then_inc(s_bar.h, 1)
            for e in (ten, gps, syn):
                e.nop().then_inc(s_bar.h, 1)
            s_bar.n += 5
            for e in (ten, vec, sca, gps, syn):
                e.wait_ge(s_bar.h, s_bar.n)

        def vec_mark(sem_):
            sem_.inc(vec.tensor_copy(bar_s[:, 0:1], bar_s[:, 1:2]))

        _pool, _pool_i = {"c": [], "sw": []}, {"c": 0, "sw": 0}

        def psem(kind="c"):
            p = _pool[kind]
            if _pool_i[kind] >= len(p):
                p.append(Sem(nc, st, f"pool_{kind}{len(p)}"))
            s_ = p[_pool_i[kind]]
            _pool_i[kind] += 1
            return s_

        def pswd():
            return psem("sw")

        def pool_reset():
            _pool_i["c"] = 0
            _pool_i["sw"] = 0

        for dst, src in ((masksb, msk), (qnw_sb, qnw),
                         (knw_sb, knw)):
            i = syn.dma_start(dst[:], src)
            s_hw.inc(i, 16)
        s_init.inc(gps.iota(iota_t[:], [[1, 128]], channel_multiplier=-1,
                            allow_small_or_imprecise_dtypes=True))
        s_init.inc(gps.memset(ones_r[:], 1.0))
        s_init.inc(gps.memset(ones_br[:], 1.0))
        s_init.inc(gps.memset(ones_c[:], 1.0))
        s_init.inc(gps.memset(negpi[:], -PI))
        s_init.inc(gps.memset(bar_s[:], 0.0))
        s_init.wait(vec)
        s_init.wait(sca)
        s_init.inc(gps.iota(ifq[:], [[0, 1]], channel_multiplier=1,
                            allow_small_or_imprecise_dtypes=True))
        s_init.wait(vec)
        s_init.inc(vec.tensor_scalar(ident_f[:], iota_t[:], 0.0, 1.0,
                                     ALU.is_equal, ALU.mult))
        s_init.wait(sca)
        s_init.inc(sca.activation(ifq[:], ifq[:], AF.Exp, scale=-LN1E4 / 32.0))

        s_tb = sem("s_tb")
        stT = ExitStack()
        ptb = stT.enter_context(nc.psum_tensor("ptb", [32, 512], F32))
        stQ = ExitStack()
        cosq = stQ.enter_context(nc.sbuf_tensor("cosq", [32, TOK], F32))
        sinq = stQ.enter_context(nc.sbuf_tensor("sinq", [32, TOK], F32))
        posq_sb = stQ.enter_context(nc.sbuf_tensor("posq_sb", [1, TOK], F32))

        def mk_tb_scratch(stx, tag):
            return (stx.enter_context(
                        nc.sbuf_tensor(f"tbred{tag}", [32, 512], F32)),
                    stx.enter_context(
                        nc.sbuf_tensor(f"tbo{tag}", [32, 512], F32)),
                    stx.enter_context(
                        nc.sbuf_tensor(f"tbk{tag}", [32, 512],
                                       mybir.dt.int32)),
                    stx.enter_context(
                        nc.sbuf_tensor(f"tbf{tag}", [32, 512], F32)))

        def build_tables_chunk(prow, cos_dst, sin_dst, scr):
            tbred, tbo, tbk, tbf = scr
            """cos/sin[i, p] = cos/sin(inv_freq[i] * pos[p]).
            Range-reduce ang - 2*pi*round(ang/2pi) via int cast + clamp
            (correct for either trunc or round-to-nearest cast), then Sin.
            dsts are [32, 512] APs."""
            s_init.wait(ten)
            s_tb.wait(ten)
            s_tb.inc(ten.matmul(ptb[:], ones_r[:, 0:32], prow,
                                start=True, stop=True))
            s_tb.wait(vec)
            s_tb.inc(vec.tensor_scalar(tbred[:], ptb[:], ifq[:, 0:1], 0.0,
                                       ALU.mult, ALU.add))
            for dst, off in ((sin_dst, 0.0), (cos_dst, PI / 2)):
                s_tb.wait(vec)
                s_tb.inc(vec.tensor_scalar(tbo[:], tbred[:], off, 0.0,
                                           ALU.add, ALU.add))
                s_tb.wait(vec)
                s_tb.inc(vec.tensor_scalar(tbf[:], tbo[:], PI, 1.0 / (2 * PI),
                                           ALU.add, ALU.mult))
                s_tb.wait(vec)
                s_tb.inc(vec.tensor_copy(tbk[:], tbf[:]))
                s_tb.wait(vec)
                s_tb.inc(vec.tensor_copy(tbf[:], tbk[:]))
                s_tb.wait(vec)
                s_tb.inc(vec.tensor_scalar(tbf[:], tbf[:], 2 * PI, 0.0,
                                           ALU.mult, ALU.add))
                s_tb.wait(vec)
                s_tb.inc(vec.tensor_tensor(dst, tbo[:], tbf[:], ALU.subtract))
                s_tb.wait(vec)
                s_tb.inc(vec.tensor_scalar(tbf[:], dst, PI, 2 * PI, ALU.is_gt,
                                           ALU.mult))
                s_tb.wait(vec)
                s_tb.inc(vec.tensor_tensor(dst, dst, tbf[:], ALU.subtract))
                s_tb.wait(vec)
                s_tb.inc(vec.tensor_scalar(tbf[:], dst, -PI, 2 * PI, ALU.is_lt,
                                           ALU.mult))
                s_tb.wait(vec)
                s_tb.inc(vec.tensor_tensor(dst, dst, tbf[:], ALU.add))
                s_tb.wait(sca)
                sca.wait_ge(s_init.h, 5)
                s_tb.inc(sca.activation(dst, dst, AF.Sin))

        scrQ = mk_tb_scratch(stQ, "Q")
        i = syn.dma_start(posq_sb[:], pos)
        s_hw.inc(i, 16)
        s_hw.wait(vec)
        s_hw.wait(ten)
        for c in range(TOK // 512):
            build_tables_chunk(posq_sb[:, c * 512:(c + 1) * 512],
                               cosq[:, c * 512:(c + 1) * 512],
                               sinq[:, c * 512:(c + 1) * 512], scrQ)

        with ExitStack() as stA:
            def sbA(name, shape, dt):
                return stA.enter_context(nc.sbuf_tensor(name, shape, dt))
            xob = sbA("xob", [128, NDC * TOK], BF16)
            wqd_sb = sbA("wqd_sb", [128, NDC * QR], BF16)
            st1 = sbA("stA1", [1, 512], F32)
            st2 = sbA("stA2", [1, 512], F32)
            st3 = sbA("stA3", [1, 512], F32)
            qcr = sbA("qcr", [128, NRC * 512], F32)
            qc_bf = sbA("qc_bf", [128, NRC * TOK], BF16)
            sqb = [sbA(f"sqA{i}", [128, 512], F32) for i in range(2)]
            wu_sl = [sbA(f"wuA{i}", [128, NRC * 128], BF16) for i in range(3)]
            wr_sl = [sbA(f"wrA{i}", [128, NRC * DR], BF16) for i in range(3)]
            stg_n = [sbA(f"stgn{i}", [128, 512], BF16) for i in range(2)]
            qpe_f = [sbA(f"qpef{i}", [32, 1024], F32) for i in range(2)]
            rtmp = sbA("rtmpA", [32, 512], F32)
            rtmp2 = sbA("rtmpA2", [32, 512], F32)
            stg_p = [sbA(f"stgp{i}", [32, 1024], BF16) for i in range(2)]
            psA = lambda n, s: stA.enter_context(nc.psum_tensor(n, s, F32))
            pqc = [psA("pqcA0", [128, 512]), psA("pqcA1", [128, 512])]
            pst = psA("pstA", [1, 512])
            pbc = psA("pbcA", [128, 512])
            ppe = psA("ppeA", [32, 1024])

            s_ldx = pswd()
            s_ldwqd = pswd()
            s_wu = [pswd() for i in range(3)]
            s_wr = [pswd() for i in range(3)]
            s_wuf = [psem() for i in range(3)]
            s_wrf = [psem() for i in range(3)]
            s_mmA = psem()
            s_evA = [psem(), psem()]
            s_sqA = [psem(), psem()]
            s_stA = psem()
            s_nrm = psem()
            s_stgn = [psem(), psem()]
            s_ppe = [psem(), psem()]
            s_rope = psem()
            s_stgp = [psem(), psem()]

            i = gps.dma_start(xob[:].rearrange("p (c t) -> p c t", c=NDC),
                              xo.rearrange("(c p) t -> p c t", p=128))
            s_ldx.inc(i, 16)
            i = gps.dma_start(wqd_sb[:].rearrange("p (c r) -> p c r", c=NDC),
                              wqd.rearrange("(c p) r -> p c r", p=128))
            s_ldwqd.inc(i, 16)

            wqu_r = wqu.rearrange("(c p) o -> p c o", p=128)
            wqr_r = wqr.rearrange("(c p) o -> p c o", p=128)

            def load_wu(i_):
                sl = i_ % 3
                s_wuf[sl].wait(gps)
                s_wu[sl].inc(
                    gps.dma_start(wu_sl[sl][:].rearrange("p (c o) -> p c o", c=NRC),
                                  wqu_r[:, :, i_ * 128:(i_ + 1) * 128]), 16)

            def load_wr(i_):
                sl = i_ % 3
                s_wrf[sl].wait(gps)
                s_wr[sl].inc(
                    gps.dma_start(wr_sl[sl][:].rearrange("p (c o) -> p c o", c=NRC),
                                  wqr_r[:, :, i_ * DR:(i_ + 1) * DR]), 16)

            for i_ in range(3):
                load_wu(i_)
                load_wr(i_)

            for tt in range(2):
                for rc in range(NRC):
                    u = rc % 2
                    s_evA[u].wait(ten)
                    s_sqA[u].wait(ten)
                    if tt == 0 and rc == 0:
                        s_ldx.wait(ten)
                        s_ldwqd.wait(ten)
                    for dc in range(NDC):
                        i = ten.matmul(
                            pqc[u][:],
                            wqd_sb[:, dc * QR + rc * 128: dc * QR + rc * 128 + 128],
                            xob[:, dc * TOK + tt * 512: dc * TOK + tt * 512 + 512],
                            start=(dc == 0), stop=(dc == NDC - 1))
                    s_mmA.inc(i)
                    s_mmA.wait(sca)
                    if tt == 1:
                        s_nrm.wait(sca)
                    s_evA[u].inc(sca.activation(qcr[:, rc * 512:(rc + 1) * 512],
                                                pqc[u][:], AF.Copy))
                    s_mmA.wait(sca)
                    s_sqA[u].wait(sca)
                    s_sqA[u].inc(sca.activation(sqb[u][:], pqc[u][:],
                                                AF.Square))
                    s_sqA[u].wait(ten)
                    if rc == 0:
                        s_stA.wait(ten)
                    i = ten.matmul(pst[:], ones_c[:], sqb[u][:],
                                   start=(rc == 0), stop=(rc == NRC - 1))
                    s_sqA[u].inc(i)
                s_sqA[(NRC - 1) % 2].wait(vec)
                s_stA.wait(vec)
                s_stA.inc(vec.tensor_scalar(st1[:], pst[:], 1.0 / QR, EPS_RMS,
                                            ALU.mult, ALU.add))
                s_stA.wait(sca)
                s_stA.inc(sca.activation(st2[:], st1[:], AF.Sqrt))
                s_stA.wait(vec)
                s_stA.inc(vec.reciprocal(st3[:], st2[:]))
                s_stA.wait(ten)
                s_nrm.wait(ten)
                s_stA.inc(ten.matmul(pbc[:], ones_r[:], st3[:], start=True,
                                     stop=True))
                s_stA.wait(vec)
                s_evA[0].wait(vec)
                s_evA[1].wait(vec)
                for rc in range(NRC):
                    s_nrm.inc(vec.tensor_tensor(qcr[:, rc * 512:(rc + 1) * 512],
                                                qcr[:, rc * 512:(rc + 1) * 512],
                                                pbc[:], ALU.mult))
                s_nrm.wait(vec)
                s_hw.wait(vec)
                for rc in range(NRC):
                    s_nrm.inc(vec.tensor_scalar(
                        qc_bf[:, rc * TOK + tt * 512: rc * TOK + tt * 512 + 512],
                        qcr[:, rc * 512:(rc + 1) * 512],
                        qnw_sb[:, rc:rc + 1], 0.0, ALU.mult, ALU.add))

            if debug:
                s_nrm.wait(gps)
                i = gps.dma_start(dbg["d_qc"], qc_bf[:])
                s_spill.inc(i, 16)

            s_nrm.wait(ten)
            s_evA[0].wait(ten)
            s_evA[1].wait(ten)
            s_sqA[0].wait(ten)
            s_sqA[1].wait(ten)
            for h in range(NH):
                sl = h % 3
                for tt in range(2):
                    u = tt % 2
                    s_stgn[u].wait(ten)
                    s_wu[sl].wait(ten)
                    for rc in range(NRC):
                        i = ten.matmul(
                            pqc[u][:], wu_sl[sl][:, rc * 128:(rc + 1) * 128],
                            qc_bf[:, rc * TOK + tt * 512: rc * TOK + tt * 512 + 512],
                            start=(rc == 0), stop=(rc == NRC - 1))
                    if tt == 1:
                        s_wuf[sl].inc(i)
                    else:
                        s_mmA.inc(i)
                    eng_sem = s_wuf[sl] if tt == 1 else s_mmA
                    eng_sem.wait(sca)
                    s_stgn[u].wait(sca)
                    s_stgn[u].inc(sca.activation(stg_n[u][:], pqc[u][:], AF.Copy))
                    s_stgn[u].wait(syn)
                    i = syn.dma_start(qspn[h, :, tt * 512:(tt + 1) * 512],
                                      stg_n[u][:])
                    s_stgn[u].inc(i, 16)
                if h + 3 < NH:
                    load_wu(h + 3)

            for h in range(NH):
                sl = h % 3
                for tt in range(2):
                    u = tt % 2
                    s_ppe[0].wait(ten)
                    s_wr[sl].wait(ten)
                    for g in range(2):
                        for rc in range(NRC):
                            i = ten.matmul(
                                ppe[:, g * 512:(g + 1) * 512],
                                wr_sl[sl][:, rc * DR + g * 32: rc * DR + g * 32 + 32],
                                qc_bf[:, rc * TOK + tt * 512: rc * TOK + tt * 512 + 512],
                                start=(rc == 0), stop=(rc == NRC - 1))
                    if tt == 1:
                        s_wrf[sl].inc(i)
                    else:
                        s_mmA.inc(i)
                    eng_sem = s_wrf[sl] if tt == 1 else s_mmA
                    eng_sem.wait(sca)
                    s_rope.wait(sca)
                    s_ppe[0].inc(sca.activation(qpe_f[u][:], ppe[:], AF.Copy))
                    s_ppe[0].wait(vec)
                    s_rope.wait(vec)
                    s_stgp[0].wait(vec)
                    s_stgp[1].wait(vec)
                    cq = cosq[:, tt * 512:(tt + 1) * 512]
                    sq = sinq[:, tt * 512:(tt + 1) * 512]
                    x1 = qpe_f[u][:, 0:512]
                    x2 = qpe_f[u][:, 512:1024]
                    s_rope.inc(vec.tensor_tensor(rtmp[:], x1, cq, ALU.mult))
                    s_rope.inc(vec.tensor_tensor(rtmp2[:], x2, sq, ALU.mult))
                    s_rope.wait(vec)
                    s_stgp[u].inc(vec.tensor_tensor(stg_p[u][:, 0:512], rtmp[:],
                                                    rtmp2[:], ALU.subtract))
                    s_stgp[u].wait(vec)
                    s_rope.inc(vec.tensor_tensor(rtmp[:], x1, sq, ALU.mult))
                    s_rope.inc(vec.tensor_tensor(rtmp2[:], x2, cq, ALU.mult))
                    s_rope.wait(vec)
                    s_stgp[u].inc(vec.tensor_tensor(stg_p[u][:, 512:1024],
                                                    rtmp[:], rtmp2[:], ALU.add))
                    s_stgp[u].wait(syn)
                    i = syn.dma_start(
                        qspp[h].rearrange("(g r) T -> r g T", g=2)[:, :, tt * 512:(tt + 1) * 512],
                        stg_p[u][:].rearrange("r (g t) -> r g t", g=2))
                    s_stgp[u].inc(i, 16)
                if h + 3 < NH:
                    load_wr(h + 3)

            for se in (s_stgn[0], s_stgn[1], s_stgp[0], s_stgp[1]):
                se.wait(syn)
            s_spill.wait(gps)
            barrier()

        stQ.close()

        stBC = ExitStack()
        kvc_bf = stBC.enter_context(nc.sbuf_tensor("kvc_bf", [128, NKC * S], BF16))
        kr_bf = stBC.enter_context(nc.sbuf_tensor("kr_bf", [DR, S], BF16))
        with ExitStack() as stB:
            def sbB(name, shape, dt):
                return stB.enter_context(nc.sbuf_tensor(name, shape, dt))
            cosk1 = sbB("cosk1", [32, 512], F32)
            sink1 = sbB("sink1", [32, 512], F32)
            poskb = sbB("poskb", [1, 512], F32)
            scrB = mk_tb_scratch(stB, "B")
            st1 = sbB("stB1", [1, 512], F32)
            st2 = sbB("stB2", [1, 512], F32)
            st3 = sbB("stB3", [1, 512], F32)
            xk = [sbB(f"xkB{i}", [128, NDC * 512], BF16) for i in range(2)]
            wkd_sb = sbB("wkd_sb", [128, NDC * KVR], BF16)
            wkr_sb = sbB("wkr_sb", [128, NDC * DR], BF16)
            kvcr = sbB("kvcr", [128, NKC * 512], F32)
            sqb = [sbB(f"sqB{i}", [128, 512], F32) for i in range(2)]
            krr = sbB("krr", [32, 1024], F32)
            krs = sbB("krs", [32, 1024], BF16)
            rtmp = sbB("rtmpB", [32, 512], F32)
            rtmp2 = sbB("rtmpB2", [32, 512], F32)
            psB = lambda n, s: stB.enter_context(nc.psum_tensor(n, s, F32))
            pkv = [psB("pkvB0", [128, 512]), psB("pkvB1", [128, 512])]
            pst = psB("pstB", [1, 512])
            pbc = psB("pbcB", [128, 512])
            pkr = psB("pkrB", [32, 1024])

            def kpos(c):
                s_tb.wait(gps)
                s_tb.inc(gps.iota(poskb[:], [[1, 512]], base=c * 512,
                                  channel_multiplier=0,
                                  allow_small_or_imprecise_dtypes=True))
                return poskb[:]

            s_ldw = pswd()
            s_xk = [pswd() for i in range(2)]
            s_xkf = [psem() for i in range(2)]
            s_mmB = psem()
            s_evB = [psem(), psem()]
            s_sqB = [psem(), psem()]
            s_stB = psem()
            s_nrmB = psem()
            s_krB = psem()
            s_ropeB = psem()
            s_krdma = psem()

            i = gps.dma_start(wkd_sb[:].rearrange("p (c r) -> p c r", c=NDC),
                              wkd.rearrange("(c p) r -> p c r", p=128))
            s_ldw.inc(i, 16)
            i = gps.dma_start(wkr_sb[:].rearrange("p (c r) -> p c r", c=NDC),
                              wkr.rearrange("(c p) r -> p c r", p=128))
            s_ldw.inc(i, 16)

            xf_r = xf.rearrange("(c p) t -> p c t", p=128)

            def load_xk(kb):
                sl = kb % 2
                s_xkf[sl].wait(gps)
                s_xk[sl].inc(
                    gps.dma_start(xk[sl][:].rearrange("p (c t) -> p c t", c=NDC),
                                  xf_r[:, :, kb * 512:(kb + 1) * 512]), 16)

            for kb in range(2):
                load_xk(kb)

            for kb in range(8):
                sl = kb % 2
                for rc in range(NKC):
                    u = rc % 2
                    s_evB[u].wait(ten)
                    s_sqB[u].wait(ten)
                    s_xk[sl].wait(ten)
                    if kb == 0 and rc == 0:
                        s_ldw.wait(ten)
                    for dc in range(NDC):
                        i = ten.matmul(
                            pkv[u][:],
                            wkd_sb[:, dc * KVR + rc * 128: dc * KVR + rc * 128 + 128],
                            xk[sl][:, dc * 512:(dc + 1) * 512],
                            start=(dc == 0), stop=(dc == NDC - 1))
                    s_mmB.inc(i)
                    s_mmB.wait(sca)
                    if kb > 0:
                        s_nrmB.wait(sca)
                    s_evB[u].inc(sca.activation(kvcr[:, rc * 512:(rc + 1) * 512],
                                                pkv[u][:], AF.Copy))
                    s_mmB.wait(sca)
                    s_sqB[u].wait(sca)
                    s_sqB[u].inc(sca.activation(sqb[u][:], pkv[u][:],
                                                AF.Square))
                    s_sqB[u].wait(ten)
                    if rc == 0:
                        s_stB.wait(ten)
                    i = ten.matmul(pst[:], ones_c[:], sqb[u][:],
                                   start=(rc == 0), stop=(rc == NKC - 1))
                    s_sqB[u].inc(i)
                s_krB.wait(ten)
                for g in range(2):
                    for dc in range(NDC):
                        i = ten.matmul(
                            pkr[:, g * 512:(g + 1) * 512],
                            wkr_sb[:, dc * DR + g * 32: dc * DR + g * 32 + 32],
                            xk[sl][:, dc * 512:(dc + 1) * 512],
                            start=(dc == 0), stop=(dc == NDC - 1))
                s_xkf[sl].inc(i)
                if kb + 2 < 8:
                    load_xk(kb + 2)
                s_sqB[(NKC - 1) % 2].wait(vec)
                s_stB.wait(vec)
                s_stB.inc(vec.tensor_scalar(st1[:], pst[:], 1.0 / KVR, EPS_RMS,
                                            ALU.mult, ALU.add))
                s_stB.wait(sca)
                s_stB.inc(sca.activation(st2[:], st1[:], AF.Sqrt))
                s_stB.wait(vec)
                s_stB.inc(vec.reciprocal(st3[:], st2[:]))
                s_stB.wait(ten)
                s_nrmB.wait(ten)
                s_stB.inc(ten.matmul(pbc[:], ones_r[:], st3[:], start=True,
                                     stop=True))
                s_stB.wait(vec)
                s_evB[0].wait(vec)
                s_evB[1].wait(vec)
                for rc in range(NKC):
                    s_nrmB.inc(vec.tensor_tensor(kvcr[:, rc * 512:(rc + 1) * 512],
                                                 kvcr[:, rc * 512:(rc + 1) * 512],
                                                 pbc[:], ALU.mult))
                s_nrmB.wait(vec)
                for rc in range(NKC):
                    s_nrmB.inc(vec.tensor_scalar(
                        kvc_bf[:, rc * S + kb * 512: rc * S + kb * 512 + 512],
                        kvcr[:, rc * 512:(rc + 1) * 512],
                        knw_sb[:, rc:rc + 1], 0.0, ALU.mult, ALU.add))
                s_ropeB.wait(vec)
                s_ropeB.wait(sca)
                build_tables_chunk(kpos(kb), cosk1[:], sink1[:], scrB)
                s_xkf[sl].wait(sca)
                s_ropeB.wait(sca)
                s_krB.inc(sca.activation(krr[:], pkr[:], AF.Copy))
                s_krB.wait(vec)
                s_tb.wait(vec)
                s_krdma.wait(vec)
                ck = cosk1[:]
                sk = sink1[:]
                x1 = krr[:, 0:512]
                x2 = krr[:, 512:1024]
                s_ropeB.inc(vec.tensor_tensor(rtmp[:], x1, ck, ALU.mult))
                s_ropeB.inc(vec.tensor_tensor(rtmp2[:], x2, sk, ALU.mult))
                s_ropeB.wait(vec)
                s_ropeB.inc(vec.tensor_tensor(krs[:, 0:512], rtmp[:], rtmp2[:],
                                              ALU.subtract))
                s_ropeB.wait(vec)
                s_ropeB.inc(vec.tensor_tensor(rtmp[:], x1, sk, ALU.mult))
                s_ropeB.inc(vec.tensor_tensor(rtmp2[:], x2, ck, ALU.mult))
                s_ropeB.wait(vec)
                s_ropeB.inc(vec.tensor_tensor(krs[:, 512:1024], rtmp[:],
                                              rtmp2[:], ALU.add))
                s_ropeB.wait(vec)
                s_ropeB.wait(syn)
                i = syn.dma_start(kr_bf[0:32, kb * 512:(kb + 1) * 512],
                                  krs[:, 0:512])
                s_krdma.inc(i, 16)
                i = syn.dma_start(kr_bf[32:64, kb * 512:(kb + 1) * 512],
                                  krs[:, 512:1024])
                s_krdma.inc(i, 16)

            s_krdma.wait(syn)
            if debug:
                s_nrmB.wait(gps)
                s_krdma.wait(gps)
                i = gps.dma_start(dbg["d_kvc"], kvc_bf[:])
                s_spill.inc(i, 16)
                i = gps.dma_start(dbg["d_kr"], kr_bf[:])
                s_spill.inc(i, 16)
                s_spill.wait(gps)
            barrier()
        pool_reset()

        stT.close()

        PTOFF = []
        _o = 0
        for kc in range(32):
            PTOFF.append(_o)
            _o += (8 - kc // 4) * 128
        NPT = _o
        PTOFF4 = [PTOFF[4 * qd] for qd in range(8)]

        def tp_count_after(gslot):
            hh_, pos = gslot // 8, gslot % 8
            t_ = 8 - pos
            return 16 * (36 * hh_ + (8 + t_) * (9 - t_) // 2)

        with ExitStack() as stC:
            def sbC(name, shape, dt):
                return stC.enter_context(nc.sbuf_tensor(name, shape, dt))
            K2 = sbC("K2", [128, 2 * S], BF16)
            V2 = [sbC(f"V2_{i}", [128, 2 * S], BF16) for i in range(2)]
            wkg = [sbC(f"wkgC{i}", [128, NKC * 512], BF16) for i in range(2)]
            S2 = [sbC("S2_0", [128, S], F32),
                  sbC("S2_1", [128, 7 * 512], F32)]
            P2 = [sbC(f"P2_{i}", [128, S], BF16) for i in range(3)]
            ostg = [sbC(f"ostg{i}", [128, TOK], BF16) for i in range(2)]
            Pt = sbC("Pt", [128, NPT], BF16)
            qn_sl = [sbC(f"qnC{i}", [128, TOK], BF16) for i in range(2)]
            qp_sl = [sbC(f"qpC{i}", [DR, TOK], BF16) for i in range(2)]
            smx = sbC("smx", [128, 8], F32)
            smn = sbC("smn", [128, 8], F32)
            ssum = sbC("ssum", [128, 8], F32)
            srcp = sbC("srcp", [128, 8], F32)
            psC = lambda n: stC.enter_context(
                nc.psum_tensor(n, [128, 1024], F32))
            pa = [psC("paC0"), psC("paC1")]
            pO = psC("pOC")

            s_qn = [pswd(), pswd()]
            s_qp = [pswd(), pswd()]
            s_wkg = [pswd(), pswd()]
            s_osp = [pswd(), pswd()]
            s_tp = sem("s_tpC")
            s_qfree = psem()
            s_wkfree = psem()
            s_mmC = psem()
            s_paC = [psem(), psem()]
            s_smax = psem()
            s_sexp = psem()
            s_sscl = psem()
            s_pvq = psem()
            s_oev = psem()
            PVQ0 = s_pvq.n

            wku_r = wku.rearrange("(c p) o -> p c o", p=128)

            def load_wkg(g):
                sl = g % 2
                s_wkfree.wait(gps)
                s_wkg[sl].inc(
                    gps.dma_start(wkg[sl][:].rearrange("p (c o) -> p c o",
                                                       c=NKC),
                                  wku_r[:, :, g * 512:(g + 1) * 512]), 16)

            def load_q(h):
                u = h % 2
                s_qfree.wait(gps)
                s_qn[u].inc(gps.dma_start(qn_sl[u][:], qspn[h]), 16)
                s_qp[u].inc(gps.dma_start(qp_sl[u][:], qspp[h]), 16)

            _rot = [0]

            def fill_bank():
                w = _rot[0] % 2
                _rot[0] += 1
                return w

            def emit_kbuild(g):
                sl = g % 2
                s_mmC.wait(ten)
                s_qfree.wait(ten)
                for hh in range(2):
                    for tq in range(4):
                        w = fill_bank()
                        s_paC[w].wait(ten)
                        if hh == 0 and tq == 0:
                            s_wkg[sl].wait(ten)
                        for rc in range(NKC):
                            for bk in range(2):
                                i = ten.matmul(
                                    pa[w][:, bk * 512:bk * 512 + 512],
                                    wkg[sl][:, rc * 512 + hh * 128:
                                            rc * 512 + hh * 128 + 128],
                                    kvc_bf[:, rc * S + tq * 1024 + bk * 512:
                                           rc * S + tq * 1024 + bk * 512 + 512],
                                    start=(rc == 0), stop=(rc == NKC - 1))
                        s_mmC.inc(i)
                        s_mmC.wait(sca)
                        i = sca.activation(
                            K2[:, hh * S + tq * 1024:
                               hh * S + tq * 1024 + 1024],
                            pa[w][:], AF.Copy)
                        s_paC[w].inc(i)

            def emit_vbuild(g):
                sl = g % 2
                vb = g % 2
                if g >= 2:
                    ten.wait_ge(s_pvq.h, PVQ0 + 8 * (2 * (g - 2) + 2))
                for f in range(8):
                    w = fill_bank()
                    s_paC[w].wait(ten)
                    if f == 0:
                        s_wkg[sl].wait(ten)
                    for kc in range(f * 4, f * 4 + 4):
                        for rc in range(NKC):
                            i = ten.matmul(
                                pa[w][:, (kc % 4) * 256:(kc % 4) * 256 + 256],
                                kvc_bf[:, rc * S + kc * 128:
                                       rc * S + kc * 128 + 128],
                                wkg[sl][:, rc * 512 + 256: rc * 512 + 512],
                                start=(rc == 0), stop=(rc == NKC - 1))
                    cov = s_wkfree if f == 7 else s_mmC
                    cov.inc(i)
                    cov.wait(vec)
                    i = vec.tensor_copy(V2[vb][:, f * 1024:(f + 1) * 1024],
                                        pa[w][:])
                    s_paC[w].inc(i)

            def emit_s(h, slots, stt):
                u = h % 2
                hh = h % 2
                smax_at, exp_at, scl_at, u2_of = stt

                def emit_exp(t):
                    c = t - 1
                    gs = 8 * h + (8 - t)
                    u2 = gs % 3
                    u2_of[t] = u2
                    sca.wait_ge(s_smax.h, smax_at[t])
                    if gs >= 3:
                        sca.wait_ge(s_tp.h, tp_count_after(gs - 3))
                    i = sca.activation(P2[u2][:, 0:t * 512],
                                       S2[t % 2][:, 0:t * 512], AF.Exp,
                                       bias=smn[:, c:c + 1], scale=SCALE,
                                       accum_out=ssum[:, c:c + 1])
                    s_sexp.inc(i)
                    exp_at[t] = s_sexp.n

                def emit_scale(t):
                    c = t - 1
                    vec.wait_ge(s_sexp.h, exp_at[t])
                    s_sscl.inc(vec.reciprocal(srcp[:, c:c + 1],
                                              ssum[:, c:c + 1]))
                    s_sscl.wait(vec)
                    s_sscl.inc(vec.tensor_scalar(
                        P2[u2_of[t]][:, 0:t * 512], P2[u2_of[t]][:, 0:t * 512],
                        srcp[:, c:c + 1], 0.0, ALU.mult, ALU.add))
                    scl_at[t] = s_sscl.n

                if slots[0] == 8:
                    s_paC[0].wait(ten)
                    s_paC[1].wait(ten)
                    s_qn[u].wait(ten)
                    s_qp[u].wait(ten)
                    s_sexp.wait(sca)
                    s_smax.wait(vec)
                    s_sscl.wait(sca)
                for t in slots:
                    qc0 = (t - 1) * 128
                    nun = (t + 1) // 2
                    SP = S2[t % 2]
                    for uu in range(nun):
                        width = min(1024, t * 512 - uu * 1024)
                        w = fill_bank()
                        s_paC[w].wait(ten)
                        for bk in range(width // 512):
                            o = uu * 1024 + bk * 512
                            ten.matmul(pa[w][:, bk * 512:bk * 512 + 512],
                                       qn_sl[u][:, qc0:qc0 + 128],
                                       K2[:, hh * S + o: hh * S + o + 512],
                                       start=True, stop=False)
                            i = ten.matmul(pa[w][:, bk * 512:bk * 512 + 512],
                                           qp_sl[u][:, qc0:qc0 + 128],
                                           kr_bf[:, o: o + 512],
                                           start=False, stop=True)
                        last = (t == 1)
                        cov = s_qfree if last else s_mmC
                        cov.inc(i)
                        base = uu * 1024
                        moff = width - 512 if uu == nun - 1 else width
                        if moff > 0:
                            cov.wait(sca)
                            if t <= 6:
                                sca.wait_ge(s_sexp.h, exp_at[t + 2])
                            i2 = sca.activation(SP[:, base:base + moff],
                                                pa[w][:, 0:moff], AF.Copy)
                            s_paC[w].inc(i2)
                        if uu == nun - 1:
                            cov.wait(vec)
                            if t <= 6:
                                vec.wait_ge(s_sexp.h, exp_at[t + 2])
                            i2 = vec.tensor_tensor(
                                SP[:, base + moff:base + width],
                                pa[w][:, moff:width], masksb[:], ALU.add)
                            s_paC[w].inc(i2)
                    c = t - 1
                    s_paC[0].wait(vec)
                    s_paC[1].wait(vec)
                    s_smax.inc(vec.reduce_max(smx[:, c:c + 1],
                                              SP[:, 0:t * 512], axis=AX))
                    s_smax.wait(vec)
                    s_smax.inc(vec.tensor_scalar(
                        smn[:, c:c + 1], smx[:, c:c + 1], -SCALE, 0.0,
                        ALU.mult, ALU.add))
                    smax_at[t] = s_smax.n
                    if t + 1 <= 8 and (t + 1) in smax_at:
                        emit_exp(t + 1)
                    if t + 2 <= 8 and (t + 2) in exp_at:
                        emit_scale(t + 2)
                if slots[-1] == 1:
                    emit_exp(1)
                    emit_scale(2)
                    emit_scale(1)

            def emit_softmax(h, scl_at, u2_of):
                tp_q = {}
                for t in range(8, 0, -1):
                    syn.wait_ge(s_sscl.h, scl_at[t])
                    for qd in range(t):
                        if h >= 1:
                            syn.wait_ge(s_pvq.h,
                                        PVQ0 + 8 * (h - 1) + qd + 1)
                        dst = Pt[:, PTOFF4[qd]:
                                 PTOFF4[qd] + 4 * (8 - qd) * 128].rearrange(
                            "k (b q) -> k b q", b=4)[
                            :, :, (t - 1 - qd) * 128:(t - qd) * 128]
                        src = P2[u2_of[t]][:, qd * 512:(qd + 1) * 512] \
                            .rearrange("q (b k) -> q b k", b=4)
                        s_tp.inc(syn.dma_start(dst, src, transpose=True), 16)
                    syn.wait_ge(s_tp.h, s_tp.n)
                    tp_q[t - 1] = s_tp.n
                return tp_q

            def emit_pv(h, tp_q):
                hh = h % 2
                vb = (h // 2) % 2
                s_paC[0].wait(ten)
                s_paC[1].wait(ten)
                s_oev.wait(ten)
                ten.wait_ge(s_tp.h, tp_q[0])
                for kc in range(32):
                    tmin = kc // 4 + 1
                    qlo = (tmin - 1) * 128
                    reg = PTOFF[kc]
                    lhsT = V2[vb][:, kc * 256 + hh * 128:
                                  kc * 256 + hh * 128 + 128]
                    if qlo < 512:
                        nA = (5 - tmin) * 128
                        ten.matmul(pO[:, qlo:512], lhsT, Pt[:, reg:reg + nA],
                                   start=(kc == 0), stop=(kc == 15))
                        i = ten.matmul(pO[:, 512:1024], lhsT,
                                       Pt[:, reg + nA: reg + nA + 512],
                                       start=(kc == 0), stop=(kc == 31))
                    else:
                        i = ten.matmul(pO[:, qlo:1024], lhsT,
                                       Pt[:, reg: reg + (9 - tmin) * 128],
                                       start=False, stop=(kc == 31))
                    if kc % 4 == 3:
                        s_pvq.inc(i)
                vec.wait_ge(s_pvq.h, PVQ0 + 8 * (h + 1))
                if h >= 2:
                    vec.wait_ge(s_osp[h % 2].h, s_osp[h % 2].n)
                s_oev.inc(vec.tensor_copy(ostg[h % 2][:], pO[:]))
                s_oev.wait(gps)
                s_osp[h % 2].inc(
                    gps.dma_start(osp[:, h * TOK:(h + 1) * TOK],
                                  ostg[h % 2][:]), 16)

            load_wkg(0)
            load_q(0)
            load_wkg(1)
            load_q(1)
            emit_kbuild(0)
            emit_vbuild(0)
            tpq = {}
            for h in range(NH):
                g = h // 2
                if h % 2 == 0 and g + 2 <= 7:
                    load_wkg(g + 2)
                stt = ({}, {}, {}, {})
                emit_s(h, [8, 7], stt)
                if h >= 1:
                    emit_pv(h - 1, tpq[h - 1])
                emit_s(h, [6, 5, 4, 3, 2, 1], stt)
                tpq[h] = emit_softmax(h, stt[2], stt[3])
                if h % 2 == 1 and g + 1 <= 7:
                    emit_kbuild(g + 1)
                    emit_vbuild(g + 1)
                if h + 2 < NH:
                    load_q(h + 2)
            emit_pv(NH - 1, tpq[NH - 1])
            barrier()
        pool_reset()

        stBC.close()

        xsum = sb("xsum", [128, NDC * TOK], F32)
        O_sb = sb("O_sb", [128, NH * TOK], BF16)
        st1 = sb("st1", [1, 512], F32)
        st2 = sb("st2", [1, 512], F32)
        st3 = sb("st3", [1, 512], F32)
        x1b = sb("x1b", [128, NDC * TOK], BF16)
        s_ln = sem("s_ln")

        def emit_ln(tag, to_x1b):
            stL = ExitStack()
            pstm = stL.enter_context(nc.psum_tensor(f"pstm{tag}", [1, 512], F32))
            pstv = stL.enter_context(nc.psum_tensor(f"pstv{tag}", [1, 512], F32))
            pbm = stL.enter_context(nc.psum_tensor(f"pbm{tag}", [128, 512], F32))
            pbv = stL.enter_context(nc.psum_tensor(f"pbv{tag}", [128, 512], F32))
            sqd = [stL.enter_context(
                nc.sbuf_tensor(f"sqD{tag}{i}", [128, 512], F32))
                for i in range(2)]
            s_sq = [sem(f"s_sqD{tag}{i}") for i in range(2)]
            s_stD = psem()
            for tt in range(2):
                for dc in range(NDC):
                    w = dc % 2
                    xsl = xsum[:, dc * TOK + tt * 512: dc * TOK + tt * 512 + 512]
                    s_sq[w].wait(sca)
                    s_ln.wait(sca)
                    s_sq[w].inc(sca.activation(sqd[w][:], xsl, AF.Square))
                    s_sq[w].wait(ten)
                    s_ln.wait(ten)
                    if dc == 0:
                        s_stD.wait(ten)
                    i = ten.matmul(pstv[:], ones_c[:], sqd[w][:],
                                   start=(dc == 0), stop=(dc == NDC - 1))
                    s_sq[w].inc(i)
                    i = ten.matmul(pstm[:], ones_c[:], xsl,
                                   start=(dc == 0), stop=(dc == NDC - 1))
                    if dc == NDC - 1:
                        s_stD.inc(i)
                    else:
                        s_sq[w].inc(i)
                s_stD.wait(vec)
                s_stD.inc(vec.tensor_scalar(st1[:], pstm[:], 1.0 / D, 0.0,
                                            ALU.mult, ALU.add))
                s_stD.inc(vec.tensor_scalar(st2[:], pstv[:], 1.0 / D, 0.0,
                                            ALU.mult, ALU.add))
                s_stD.wait(vec)
                s_stD.inc(vec.tensor_tensor(st3[:], st1[:], st1[:], ALU.mult))
                s_stD.wait(vec)
                s_stD.inc(vec.tensor_tensor(st2[:], st2[:], st3[:],
                                            ALU.subtract))
                s_stD.wait(vec)
                s_stD.inc(vec.tensor_scalar(st2[:], st2[:], 1.0, EPS_LN,
                                            ALU.mult, ALU.add))
                s_stD.wait(sca)
                s_stD.inc(sca.activation(st2[:], st2[:], AF.Sqrt))
                s_stD.wait(vec)
                s_stD.inc(vec.reciprocal(st2[:], st2[:]))
                s_stD.wait(ten)
                s_ln.wait(ten)
                i = ten.matmul(pbm[:], ones_r[:], st1[:], start=True, stop=True)
                s_stD.inc(i)
                i = ten.matmul(pbv[:], ones_r[:], st2[:], start=True, stop=True)
                s_stD.inc(i)
                s_stD.wait(vec)
                for dc in range(NDC):
                    xsl = xsum[:, dc * TOK + tt * 512: dc * TOK + tt * 512 + 512]
                    s_ln.inc(vec.tensor_tensor(xsl, xsl, pbm[:], ALU.subtract))
                s_ln.wait(vec)
                for dc in range(NDC):
                    xsl = xsum[:, dc * TOK + tt * 512: dc * TOK + tt * 512 + 512]
                    s_ln.inc(vec.tensor_tensor(xsl, xsl, pbv[:], ALU.mult))
                if to_x1b:
                    s_ln.wait(vec)
                    for dc in range(NDC):
                        s_ln.inc(vec.tensor_copy(
                            x1b[:, dc * TOK + tt * 512: dc * TOK + tt * 512 + 512],
                            xsum[:, dc * TOK + tt * 512: dc * TOK + tt * 512 + 512]))
            stL.close()

        with ExitStack() as stD:
            def sbD(name, shape, dt):
                return stD.enter_context(nc.sbuf_tensor(name, shape, dt))
            wo_sl = [sbD(f"woD{i}", [128, NDC * 128], BF16) for i in range(3)]
            s_wo = [pswd() for i in range(3)]
            s_wof = [psem() for i in range(3)]
            s_ldxs = psem()
            s_mmD = psem()
            s_evD = [psem(), psem()]
            pd = [stD.enter_context(nc.psum_tensor(f"pdD{i}", [128, 512], F32))
                  for i in range(2)]

            i = syn.dma_start(xsum[:].rearrange("p (c t) -> p c t", c=NDC),
                              xo.rearrange("(c p) t -> p c t", p=128))
            s_ldxs.inc(i, 16)
            s_ldo = pswd()
            s_osp[0].wait(gps)
            s_osp[1].wait(gps)
            s_ldo.inc(gps.dma_start(O_sb[:], osp), 16)
            if debug:
                s_ldo.wait(gps)
                i = gps.dma_start(dbg["d_obf"], O_sb[:])
                s_spill.inc(i, 16)
            wo_r = wo_.rearrange("(c p) o -> p c o", p=128)

            def load_wo(i_):
                sl = i_ % 3
                s_wof[sl].wait(gps)
                s_wo[sl].inc(
                    gps.dma_start(wo_sl[sl][:].rearrange("p (c o) -> p c o", c=NDC),
                                  wo_r[:, :, i_ * 128:(i_ + 1) * 128]), 16)

            for i_ in range(3):
                load_wo(i_)

            for oc in range(NDC):
                sl = oc % 3
                for tt in range(2):
                    w = tt % 2
                    s_evD[w].wait(ten)
                    s_wo[sl].wait(ten)
                    if oc == 0 and tt == 0:
                        s_ldo.wait(ten)
                    for ic in range(NDC):
                        i = ten.matmul(
                            pd[w][:], wo_sl[sl][:, ic * 128:(ic + 1) * 128],
                            O_sb[:, ic * TOK + tt * 512: ic * TOK + tt * 512 + 512],
                            start=(ic == 0), stop=(ic == NDC - 1))
                    if tt == 1:
                        s_wof[sl].inc(i)
                    else:
                        s_mmD.inc(i)
                    eng_sem = s_wof[sl] if tt == 1 else s_mmD
                    eng_sem.wait(vec)
                    if oc == 0 and tt == 0:
                        s_ldxs.wait(vec)
                    xsl = xsum[:, oc * TOK + tt * 512: oc * TOK + tt * 512 + 512]
                    s_evD[w].inc(vec.tensor_tensor(xsl, xsl, pd[w][:], ALU.add))
                if oc + 3 < NDC:
                    load_wo(oc + 3)
            s_evD[0].wait(vec)
            s_evD[1].wait(vec)
            vec_mark(s_ln)
            emit_ln("1", True)
            if debug:
                s_ln.wait(gps)
                i = gps.dma_start(dbg["d_x1"], xsum[:])
                s_spill.inc(i, 16)
                s_spill.wait(gps)
            barrier()
        pool_reset()

        with ExitStack() as stE:
            def sbE(name, shape, dt):
                return stE.enter_context(nc.sbuf_tensor(name, shape, dt))
            comb_bc = O_sb[:, 10240:10240 + TOK]
            h_buf = O_sb
            gs = [sbE(f"gsE{i}", [128, 512], BF16) for i in range(2)]
            us = [sbE(f"usE{i}", [128, 512], BF16) for i in range(2)]
            tmp_r = sbE("tmp_r", [128, 512], F32)
            wg_sl = [sbE(f"wgE{i}", [128, NDC * 128], BF16) for i in range(2)]
            wu_sl = [sbE(f"wuE{i}", [128, NDC * 128], BF16) for i in range(2)]
            RD = 8
            wd_sl = [sbE(f"wdE{i}", [128, D], BF16) for i in range(RD)]
            stRt = ExitStack()
            sbR = lambda n, s, d: stRt.enter_context(nc.sbuf_tensor(n, s, d))
            rw_sb = sbR("rw_sb", [128, NDC * 8], F32)
            lg = sbR("lg", [8, TOK], F32)
            lt = sbR("lt", [128, 64], F32)
            e1 = sbR("e1", [128, 8], F32)
            m1 = sbR("m1", [128, 1], F32)
            m2 = sbR("m2", [128, 1], F32)
            cc = sbR("cc", [128, 8], F32)
            crow = O_sb[0:1, 9216:9216 + TOK]
            stR = ExitStack()
            pr = stR.enter_context(nc.psum_tensor("prE", [8, 512], F32))
            pct = stR.enter_context(nc.psum_tensor("pctE", [1, 128], F32))
            pcb = stR.enter_context(nc.psum_tensor("pcbE", [128, 512], F32))
            ptr = stR.enter_context(nc.psum_tensor("ptrE", [128, 8], F32))

            s_ldr = psem()
            s_mmE = psem()
            s_rt = psem()
            s_cmb = psem()
            s_crw = psem()
            s_cbc = psem()
            s_wg = [pswd() for i in range(2)]
            s_wgf = [psem() for i in range(2)]
            s_wu = [pswd() for i in range(2)]
            s_wuf = [psem() for i in range(2)]
            s_wd = [pswd() for i in range(RD)]
            s_wdf = [psem() for i in range(RD)]
            s_gs = [psem(), psem()]
            s_us = [psem(), psem()]
            s_hw2 = psem()
            s_dn = [psem(), psem()]
            s_acc = psem()
            s_tmpr = psem()
            s_out = psem()

            i = syn.dma_start(rw_sb[:].rearrange("p (c o) -> p c o", c=NDC),
                              rw.rearrange("(c p) o -> p c o", p=128))
            s_ldr.inc(i, 16)
            for tt in range(2):
                s_ldr.wait(ten)
                s_rt.wait(ten)
                for dc in range(NDC):
                    i = ten.matmul(
                        pr[:], rw_sb[:, dc * 8:(dc + 1) * 8],
                        xsum[:, dc * TOK + tt * 512: dc * TOK + tt * 512 + 512],
                        start=(dc == 0), stop=(dc == NDC - 1))
                s_mmE.inc(i)
                s_mmE.wait(sca)
                s_rt.inc(sca.activation(lg[:, tt * 512:(tt + 1) * 512], pr[:],
                                        AF.Copy))
            for ch in range(8):
                s_rt.wait(ten)
                s_cmb.wait(ten)
                i = ten.transpose(ptr[:], lg[:, ch * 128:(ch + 1) * 128],
                                  ident_f[0:8, 0:8])
                s_rt.inc(i)
                s_rt.wait(vec)
                s_cmb.inc(vec.tensor_copy(lt[:, ch * 8:(ch + 1) * 8], ptr[:]))
                ltc = lt[:, ch * 8:(ch + 1) * 8]
                s_cmb.wait(vec)
                s_cmb.inc(vec.reduce_max(m1[:], ltc, axis=AX))
                s_cmb.wait(vec)
                s_cmb.inc(vec.tensor_scalar(e1[:], ltc, m1[:, 0:1], 1e30,
                                            ALU.is_ge, ALU.mult))
                s_cmb.wait(vec)
                s_cmb.inc(vec.tensor_tensor(e1[:], ltc, e1[:], ALU.subtract))
                s_cmb.wait(vec)
                s_cmb.inc(vec.reduce_max(m2[:], e1[:], axis=AX))
                s_cmb.wait(vec)
                s_cmb.inc(vec.tensor_tensor(m2[:], m2[:], m1[:], ALU.subtract))
                s_cmb.wait(sca)
                s_cmb.inc(sca.activation(m2[:], m2[:], AF.Exp))
                s_cmb.wait(vec)
                s_cmb.inc(vec.tensor_scalar(m1[:], m2[:], 1.0, 0.0, ALU.add,
                                            ALU.add))
                s_cmb.wait(vec)
                s_cmb.inc(vec.reciprocal(m1[:], m1[:]))
                s_cmb.wait(vec)
                s_cmb.inc(vec.tensor_tensor(m2[:], m2[:], m1[:], ALU.mult))
                s_cmb.wait(vec)
                s_cmb.inc(vec.tensor_tensor(cc[:, ch:ch + 1], m1[:], m2[:],
                                            ALU.add))
            for ch in range(8):
                s_cmb.wait(ten)
                s_crw.wait(ten)
                i = ten.transpose(pct[:], cc[:, ch:ch + 1], ident_f[:])
                s_crw.inc(i)
                s_crw.wait(vec)
                s_crw.inc(vec.tensor_copy(crow[:, ch * 128:(ch + 1) * 128],
                                          pct[:]))
            for tt in range(2):
                s_crw.wait(ten)
                s_cbc.wait(ten)
                i = ten.matmul(pcb[:], ones_br[:],
                               crow[:, tt * 512:(tt + 1) * 512], start=True,
                               stop=True)
                s_cbc.inc(i)
                s_cbc.wait(vec)
                s_cbc.inc(vec.tensor_copy(comb_bc[:, tt * 512:(tt + 1) * 512],
                                          pcb[:]))

            stR.close()
            stRt.close()
            stM = ExitStack()
            pg = [stM.enter_context(nc.psum_tensor(f"pgE{i}", [128, 512], F32))
                  for i in range(2)]
            pu = [stM.enter_context(nc.psum_tensor(f"puE{i}", [128, 512], F32))
                  for i in range(2)]
            pdn = [stM.enter_context(nc.psum_tensor(f"pdnE{i}", [128, 512], F32))
                   for i in range(2)]
            for se_ in (s_cbc, s_cmb, s_crw, s_rt):
                se_.wait(ten)

            sg_r = sg.rearrange("(c p) o -> p c o", p=128)
            su_r = su.rearrange("(c p) o -> p c o", p=128)
            rg_r = rg.rearrange("(c p) o -> p c o", p=128)
            ru_r = ru.rearrange("(c p) o -> p c o", p=128)

            def load_gu(gi):
                gsrc, usrc, ii = (sg_r, su_r, gi) if gi < 64 else (rg_r, ru_r,
                                                                   gi - 64)
                sl = gi % 2
                s_wgf[sl].wait(gps)
                s_wg[sl].inc(
                    gps.dma_start(wg_sl[sl][:].rearrange("p (c o) -> p c o", c=NDC),
                                  gsrc[:, :, ii * 128:(ii + 1) * 128]), 16)
                s_wuf[sl].wait(gps)
                s_wu[sl].inc(
                    gps.dma_start(wu_sl[sl][:].rearrange("p (c o) -> p c o", c=NDC),
                                  usrc[:, :, ii * 128:(ii + 1) * 128]), 16)

            def load_wd(gi):
                dsrc, ii = (sd, gi) if gi < 64 else (rd, gi - 64)
                sl = gi % RD
                s_wdf[sl].wait(gps)
                s_wd[sl].inc(
                    gps.dma_start(wd_sl[sl][:],
                                  dsrc[ii * 128:(ii + 1) * 128, :]), 16)

            for gi in range(2):
                load_gu(gi)
            for gi in range(RD):
                load_wd(gi)

            NGI = 72
            for gi in range(NGI):
                grp, icg = gi // 8, gi % 8
                sl = gi % 2
                for tt in range(2):
                    w = tt % 2
                    s_gs[w].wait(ten)
                    s_wg[sl].wait(ten)
                    for dc in range(NDC):
                        i = ten.matmul(
                            pg[w][:], wg_sl[sl][:, dc * 128:(dc + 1) * 128],
                            x1b[:, dc * TOK + tt * 512: dc * TOK + tt * 512 + 512],
                            start=(dc == 0), stop=(dc == NDC - 1))
                    (s_wgf[sl] if tt == 1 else s_mmE).inc(i)
                    s_us[w].wait(ten)
                    s_wu[sl].wait(ten)
                    for dc in range(NDC):
                        i = ten.matmul(
                            pu[w][:], wu_sl[sl][:, dc * 128:(dc + 1) * 128],
                            x1b[:, dc * TOK + tt * 512: dc * TOK + tt * 512 + 512],
                            start=(dc == 0), stop=(dc == NDC - 1))
                    (s_wuf[sl] if tt == 1 else s_mmE).inc(i)
                    (s_wgf[sl] if tt == 1 else s_mmE).wait(sca)
                    s_gs[w].wait(sca)
                    s_gs[w].inc(sca.activation(gs[w][:], pg[w][:], AF.Silu))
                    (s_wuf[sl] if tt == 1 else s_mmE).wait(vec)
                    s_us[w].wait(vec)
                    s_gs[w].wait(vec)
                    s_us[w].inc(vec.tensor_copy(us[w][:], pu[w][:]))
                    s_gs[w].wait(vec)
                    s_us[w].wait(vec)
                    if grp > 0:
                        s_wdf[(gi - 8) % RD].wait(vec)
                    s_gs[w].inc(vec.tensor_tensor(
                        h_buf[:, icg * TOK + tt * 512: icg * TOK + tt * 512 + 512],
                        gs[w][:], us[w][:], ALU.mult))
                if gi + 2 < NGI:
                    load_gu(gi + 2)
                if icg == 7:
                    routed = grp == 8
                    s_gs[0].wait(ten)
                    s_gs[1].wait(ten)
                    for oc in range(NDC):
                        for tt in range(2):
                            w = tt % 2
                            s_dn[w].wait(ten)
                            if oc == 0:
                                for ic2 in range(8):
                                    s_wd[(grp * 8 + ic2) % RD].wait(ten)
                            for ic2 in range(8):
                                gi2 = grp * 8 + ic2
                                dsl = gi2 % RD
                                i = ten.matmul(
                                    pdn[w][:], wd_sl[dsl][:, oc * 128:(oc + 1) * 128],
                                    h_buf[:, ic2 * TOK + tt * 512: ic2 * TOK + tt * 512 + 512],
                                    start=(ic2 == 0), stop=(ic2 == 7))
                                if oc == NDC - 1 and tt == 1:
                                    s_wdf[dsl].inc(i)
                            if not (oc == NDC - 1 and tt == 1):
                                s_mmE.inc(i)
                            cover = s_wdf[(grp * 8 + 7) % RD] if (
                                oc == NDC - 1 and tt == 1) else s_mmE
                            cover.wait(vec)
                            xsl = xsum[:, oc * TOK + tt * 512: oc * TOK + tt * 512 + 512]
                            if routed:
                                s_acc.wait(vec)
                                s_cbc.wait(vec)
                                s_dn[w].inc(vec.tensor_tensor(
                                    tmp_r[:], pdn[w][:],
                                    comb_bc[:, tt * 512:(tt + 1) * 512],
                                    ALU.mult))
                                s_dn[w].wait(vec)
                                s_acc.inc(vec.tensor_tensor(xsl, xsl, tmp_r[:],
                                                            ALU.add))
                            else:
                                s_dn[w].inc(vec.tensor_tensor(xsl, xsl,
                                                              pdn[w][:],
                                                              ALU.add))
                    for ic2 in range(8):
                        if grp * 8 + 8 + ic2 < NGI:
                            load_wd(grp * 8 + 8 + ic2)

            stM.close()
            s_acc.wait(vec)
            s_dn[0].wait(vec)
            s_dn[1].wait(vec)
            vec_mark(s_ln)
            emit_ln("2", False)
            s_ln.wait(syn)
            i = syn.dma_start(out_.rearrange("(c p) t -> p c t", p=128),
                              xsum[:].rearrange("p (c t) -> p c t", c=NDC))
            s_out.inc(i, 16)
            s_out.wait(syn)
    return nc


def _deint(nh):
    p = np.concatenate([np.arange(0, 64, 2), np.arange(1, 64, 2)])
    return np.concatenate([h * 64 + p for h in range(nh)])


def _wku_perm():
    idx = []
    for g in range(NH // 2):
        h0, h1 = 2 * g, 2 * g + 1
        idx += [np.arange(h0 * 256, h0 * 256 + 128),
                np.arange(h1 * 256, h1 * 256 + 128),
                np.arange(h0 * 256 + 128, h0 * 256 + 256),
                np.arange(h1 * 256 + 128, h1 * 256 + 256)]
    return np.concatenate(idx)


def kernel(**inp):
    x = np.ascontiguousarray(inp["L0_attn_x"], dtype=np.float32)
    T = lambda a: np.ascontiguousarray(np.asarray(a, dtype=np.float32).T)
    shared = {
        "wqd": T(inp["L0_attn_wq_down"]),
        "wqu": T(inp["L0_attn_wq_up"]),
        "wqr": np.ascontiguousarray(T(inp["L0_attn_wq_rope"])[:, _deint(NH)]),
        "qnw": np.ascontiguousarray(
            np.asarray(inp["L0_attn_q_norm_w"], np.float32).reshape(NRC, 128).T),
        "wkd": T(inp["L0_attn_wkv_down"]),
        "knw": np.ascontiguousarray(
            np.asarray(inp["L0_attn_kv_norm_w"], np.float32).reshape(NKC, 128).T),
        "wku": np.ascontiguousarray(T(inp["L0_attn_wkv_up"])[:, _wku_perm()]),
        "wkr": np.ascontiguousarray(T(inp["L0_attn_wk_rope"])[:, _deint(1)]),
        "wo": T(inp["L0_attn_wo"]),
        "rw": T(inp["L0_ffn_router_weight"]),
        "sg": T(inp["L0_ffn_sh_gate"]),
        "su": T(inp["L0_ffn_sh_up"]),
        "sd": T(inp["L0_ffn_sh_down"]),
        "rg": T(inp["L0_ffn_r_gate"]),
        "ru": T(inp["L0_ffn_r_up"]),
        "rd": T(inp["L0_ffn_r_down"]),
    }
    in_maps = []
    cols_l = []
    for core in range(8):
        b, j = core // 4, core % 4
        chunks = [4 * (t - 1) + j for t in range(1, 9)]
        cols = np.concatenate([np.arange(c * 128, c * 128 + 128) for c in chunks])
        cols_l.append((b, cols))
        xT = x[b].T
        m = dict(shared)
        m["xo"] = np.ascontiguousarray(xT[:, cols])
        m["xf"] = np.ascontiguousarray(xT)
        m["msk"] = np.where(
            np.arange(512)[None, :] <= 128 * j + np.arange(128)[:, None],
            np.float32(0), np.float32(-1e30)).astype(np.float32)
        m["pos"] = cols.astype(np.float32).reshape(1, TOK)
        in_maps.append(m)

    nc = build_nc()
    res = run_bass_kernel_spmd(nc, in_maps, core_ids=list(range(8)))
    full = np.zeros((2, S, D), dtype=np.float32)
    for core in range(8):
        b, cols = cols_l[core]
        full[b, cols, :] = np.asarray(res.results[core]["out"]).T
    return full


# version 08
# speedup vs baseline: 1.0853x; 1.0853x over previous
import numpy as np
import ml_dtypes
from contextlib import ExitStack

import concourse.bass as bass
import concourse.mybir as mybir
from concourse.bass_utils import run_bass_kernel_spmd

F32 = mybir.dt.float32
BF16 = mybir.dt.bfloat16
AX = mybir.AxisListType.X
AF = mybir.ActivationFunctionType
ALU = mybir.AluOpType

NH, DN, DR, DV = 16, 128, 64, 128
QR, KVR, D = 1536, 512, 2048
S = 4096
TOK = 1024
EPS_RMS, EPS_LN = 1e-6, 1e-5
SCALE = float(1.0 / np.sqrt(DN + DR))
PI = float(np.pi)
LN1E4 = float(np.log(10000.0))
NDC = D // 128
NRC = QR // 128
NKC = KVR // 128
POFF = [512 * t * (t - 1) // 2 for t in range(1, 10)]


class Sem:
    __slots__ = ("h", "n")

    def __init__(self, nc, st, name):
        self.h = st.enter_context(nc.semaphore(name))
        self.n = 0

    def inc(self, inst, k=1):
        inst.then_inc(self.h, k)
        self.n += k

    def wait(self, eng):
        if self.n:
            eng.wait_ge(self.h, self.n)


def build_nc(debug=False):
    nc = bass.Bass("TRN2", target_bir_lowering=False)
    ten, vec, sca, gps, syn = nc.tensor, nc.vector, nc.scalar, nc.gpsimd, nc.sync

    def din(name, shape):
        return nc.dram_tensor(name, shape, F32, kind="ExternalInput").ap()

    xo = din("xo", [D, TOK])
    xf = din("xf", [D, S])
    wqd = din("wqd", [D, QR])
    wqu = din("wqu", [QR, D])
    wqr = din("wqr", [QR, NH * DR])
    qnw = din("qnw", [128, NRC])
    wkd = din("wkd", [D, KVR])
    knw = din("knw", [128, NKC])
    wku = din("wku", [KVR, NH * 256])
    wkr = din("wkr", [D, DR])
    wo_ = din("wo", [D, D])
    rw = din("rw", [D, 8])
    sg = din("sg", [D, 8192])
    su = din("su", [D, 8192])
    sd = din("sd", [8192, D])
    rg = din("rg", [D, 1024])
    ru = din("ru", [D, 1024])
    rd = din("rd", [1024, D])
    msk = din("msk", [128, 512])
    pos = din("pos", [1, TOK])
    out_ = nc.dram_tensor("out", [D, TOK], F32, kind="ExternalOutput").ap()
    qspn = nc.dram_tensor("qspn", [NH, 128, TOK], BF16, kind="Internal").ap()
    qspp = nc.dram_tensor("qspp", [NH, DR, TOK], BF16, kind="Internal").ap()
    osp = nc.dram_tensor("osp", [128, NH * TOK], BF16, kind="Internal").ap()

    dbg = {}
    if debug:
        for nm, shp in [("d_qc", [128, NRC * TOK]), ("d_kvc", [128, NKC * S]),
                        ("d_kr", [DR, S]), ("d_obf", [128, NH * TOK]),
                        ("d_x1", [128, NDC * TOK])]:
            dbg[nm] = nc.dram_tensor(nm, shp, F32, kind="ExternalOutput").ap()

    st = ExitStack()
    with st:
        def sem(name):
            return Sem(nc, st, name)

        def sb(name, shape, dt):
            return st.enter_context(nc.sbuf_tensor(name, shape, dt))

        def ps(name, shape):
            return st.enter_context(nc.psum_tensor(name, shape, F32))

        ident_f = sb("ident_f", [128, 128], F32)
        iota_t = sb("iota_t", [128, 128], F32)
        ones_r = sb("ones_r", [1, 128], F32)
        ones_br = sb("ones_br", [1, 128], BF16)
        ones_c = sb("ones_c", [128, 1], F32)
        masksb = sb("masksb", [128, 512], F32)
        ifq = sb("ifq", [32, 1], F32)
        qnw_sb = sb("qnw_sb", [128, NRC], F32)
        knw_sb = sb("knw_sb", [128, NKC], F32)
        negpi = sb("negpi", [128, 1], F32)
        bar_s = sb("bar_s", [1, 4], F32)

        s_init = sem("s_init")
        s_hw = sem("s_hw")
        s_bar = sem("s_bar")
        s_spill = sem("s_spill")

        def barrier():
            vec.tensor_copy(bar_s[:, 0:1], bar_s[:, 1:2]).then_inc(s_bar.h, 1)
            sca.activation(bar_s[:, 2:3], bar_s[:, 1:2],
                           AF.Copy).then_inc(s_bar.h, 1)
            for e in (ten, gps, syn):
                e.nop().then_inc(s_bar.h, 1)
            s_bar.n += 5
            for e in (ten, vec, sca, gps, syn):
                e.wait_ge(s_bar.h, s_bar.n)

        def vec_mark(sem_):
            sem_.inc(vec.tensor_copy(bar_s[:, 0:1], bar_s[:, 1:2]))

        _pool, _pool_i = {"c": [], "sw": []}, {"c": 0, "sw": 0}

        def psem(kind="c"):
            p = _pool[kind]
            if _pool_i[kind] >= len(p):
                p.append(Sem(nc, st, f"pool_{kind}{len(p)}"))
            s_ = p[_pool_i[kind]]
            _pool_i[kind] += 1
            return s_

        def pswd():
            return psem("sw")

        def pool_reset():
            _pool_i["c"] = 0
            _pool_i["sw"] = 0

        for dst, src in ((masksb, msk), (qnw_sb, qnw),
                         (knw_sb, knw)):
            i = syn.dma_start(dst[:], src)
            s_hw.inc(i, 16)
        s_init.inc(gps.iota(iota_t[:], [[1, 128]], channel_multiplier=-1,
                            allow_small_or_imprecise_dtypes=True))
        s_init.inc(gps.memset(ones_r[:], 1.0))
        s_init.inc(gps.memset(ones_br[:], 1.0))
        s_init.inc(gps.memset(ones_c[:], 1.0))
        s_init.inc(gps.memset(negpi[:], -PI))
        s_init.inc(gps.memset(bar_s[:], 0.0))
        s_init.wait(vec)
        s_init.wait(sca)
        s_init.inc(gps.iota(ifq[:], [[0, 1]], channel_multiplier=1,
                            allow_small_or_imprecise_dtypes=True))
        s_init.wait(vec)
        s_init.inc(vec.tensor_scalar(ident_f[:], iota_t[:], 0.0, 1.0,
                                     ALU.is_equal, ALU.mult))
        s_init.wait(sca)
        s_init.inc(sca.activation(ifq[:], ifq[:], AF.Exp, scale=-LN1E4 / 32.0))

        s_tb = sem("s_tb")
        stT = ExitStack()
        ptb = stT.enter_context(nc.psum_tensor("ptb", [32, 512], F32))
        stQ = ExitStack()
        cosq = stQ.enter_context(nc.sbuf_tensor("cosq", [32, TOK], F32))
        sinq = stQ.enter_context(nc.sbuf_tensor("sinq", [32, TOK], F32))
        posq_sb = stQ.enter_context(nc.sbuf_tensor("posq_sb", [1, TOK], F32))

        def mk_tb_scratch(stx, tag):
            return (stx.enter_context(
                        nc.sbuf_tensor(f"tbred{tag}", [32, 512], F32)),
                    stx.enter_context(
                        nc.sbuf_tensor(f"tbo{tag}", [32, 512], F32)),
                    stx.enter_context(
                        nc.sbuf_tensor(f"tbk{tag}", [32, 512],
                                       mybir.dt.int32)),
                    stx.enter_context(
                        nc.sbuf_tensor(f"tbf{tag}", [32, 512], F32)))

        def build_tables_chunk(prow, cos_dst, sin_dst, scr):
            tbred, tbo, tbk, tbf = scr
            """cos/sin[i, p] = cos/sin(inv_freq[i] * pos[p]).
            Range-reduce ang - 2*pi*round(ang/2pi) via int cast + clamp
            (correct for either trunc or round-to-nearest cast), then Sin.
            dsts are [32, 512] APs."""
            s_init.wait(ten)
            s_tb.wait(ten)
            s_tb.inc(ten.matmul(ptb[:], ones_r[:, 0:32], prow,
                                start=True, stop=True))
            s_tb.wait(vec)
            s_tb.inc(vec.tensor_scalar(tbred[:], ptb[:], ifq[:, 0:1], 0.0,
                                       ALU.mult, ALU.add))
            for dst, off in ((sin_dst, 0.0), (cos_dst, PI / 2)):
                s_tb.wait(vec)
                s_tb.inc(vec.tensor_scalar(tbo[:], tbred[:], off, 0.0,
                                           ALU.add, ALU.add))
                s_tb.wait(vec)
                s_tb.inc(vec.tensor_scalar(tbf[:], tbo[:], PI, 1.0 / (2 * PI),
                                           ALU.add, ALU.mult))
                s_tb.wait(vec)
                s_tb.inc(vec.tensor_copy(tbk[:], tbf[:]))
                s_tb.wait(vec)
                s_tb.inc(vec.tensor_copy(tbf[:], tbk[:]))
                s_tb.wait(vec)
                s_tb.inc(vec.tensor_scalar(tbf[:], tbf[:], 2 * PI, 0.0,
                                           ALU.mult, ALU.add))
                s_tb.wait(vec)
                s_tb.inc(vec.tensor_tensor(dst, tbo[:], tbf[:], ALU.subtract))
                s_tb.wait(vec)
                s_tb.inc(vec.tensor_scalar(tbf[:], dst, PI, 2 * PI, ALU.is_gt,
                                           ALU.mult))
                s_tb.wait(vec)
                s_tb.inc(vec.tensor_tensor(dst, dst, tbf[:], ALU.subtract))
                s_tb.wait(vec)
                s_tb.inc(vec.tensor_scalar(tbf[:], dst, -PI, 2 * PI, ALU.is_lt,
                                           ALU.mult))
                s_tb.wait(vec)
                s_tb.inc(vec.tensor_tensor(dst, dst, tbf[:], ALU.add))
                s_tb.wait(sca)
                sca.wait_ge(s_init.h, 5)
                s_tb.inc(sca.activation(dst, dst, AF.Sin))

        scrQ = mk_tb_scratch(stQ, "Q")
        i = syn.dma_start(posq_sb[:], pos)
        s_hw.inc(i, 16)
        s_hw.wait(vec)
        s_hw.wait(ten)
        for c in range(TOK // 512):
            build_tables_chunk(posq_sb[:, c * 512:(c + 1) * 512],
                               cosq[:, c * 512:(c + 1) * 512],
                               sinq[:, c * 512:(c + 1) * 512], scrQ)

        with ExitStack() as stA:
            def sbA(name, shape, dt):
                return stA.enter_context(nc.sbuf_tensor(name, shape, dt))
            xob = sbA("xob", [128, NDC * TOK], BF16)
            wqd_sb = sbA("wqd_sb", [128, NDC * QR], BF16)
            st1 = sbA("stA1", [1, 512], F32)
            st2 = sbA("stA2", [1, 512], F32)
            st3 = sbA("stA3", [1, 512], F32)
            qcr = sbA("qcr", [128, NRC * 512], F32)
            qc_bf = sbA("qc_bf", [128, NRC * TOK], BF16)
            sqb = [sbA(f"sqA{i}", [128, 512], F32) for i in range(2)]
            wu_sl = [sbA(f"wuA{i}", [128, NRC * 128], BF16) for i in range(3)]
            wr_sl = [sbA(f"wrA{i}", [128, NRC * DR], BF16) for i in range(3)]
            stg_n = [sbA(f"stgn{i}", [128, 512], BF16) for i in range(2)]
            qpe_f = [sbA(f"qpef{i}", [32, 1024], F32) for i in range(2)]
            rtmp = sbA("rtmpA", [32, 512], F32)
            rtmp2 = sbA("rtmpA2", [32, 512], F32)
            stg_p = [sbA(f"stgp{i}", [32, 1024], BF16) for i in range(2)]
            psA = lambda n, s: stA.enter_context(nc.psum_tensor(n, s, F32))
            pqc = [psA("pqcA0", [128, 512]), psA("pqcA1", [128, 512])]
            pst = psA("pstA", [1, 512])
            pbc = psA("pbcA", [128, 512])
            ppe = psA("ppeA", [32, 1024])

            s_ldx = pswd()
            s_ldwqd = pswd()
            s_wu = [pswd() for i in range(3)]
            s_wr = [pswd() for i in range(3)]
            s_wuf = [psem() for i in range(3)]
            s_wrf = [psem() for i in range(3)]
            s_mmA = psem()
            s_evA = [psem(), psem()]
            s_sqA = [psem(), psem()]
            s_stA = psem()
            s_nrm = psem()
            s_stgn = [psem(), psem()]
            s_ppe = [psem(), psem()]
            s_rope = psem()
            s_stgp = [psem(), psem()]

            i = gps.dma_start(xob[:].rearrange("p (c t) -> p c t", c=NDC),
                              xo.rearrange("(c p) t -> p c t", p=128))
            s_ldx.inc(i, 16)
            i = gps.dma_start(wqd_sb[:].rearrange("p (c r) -> p c r", c=NDC),
                              wqd.rearrange("(c p) r -> p c r", p=128))
            s_ldwqd.inc(i, 16)

            wqu_r = wqu.rearrange("(c p) o -> p c o", p=128)
            wqr_r = wqr.rearrange("(c p) o -> p c o", p=128)

            def load_wu(i_):
                sl = i_ % 3
                s_wuf[sl].wait(gps)
                s_wu[sl].inc(
                    gps.dma_start(wu_sl[sl][:].rearrange("p (c o) -> p c o", c=NRC),
                                  wqu_r[:, :, i_ * 128:(i_ + 1) * 128]), 16)

            def load_wr(i_):
                sl = i_ % 3
                s_wrf[sl].wait(gps)
                s_wr[sl].inc(
                    gps.dma_start(wr_sl[sl][:].rearrange("p (c o) -> p c o", c=NRC),
                                  wqr_r[:, :, i_ * DR:(i_ + 1) * DR]), 16)

            for i_ in range(3):
                load_wu(i_)
                load_wr(i_)

            for tt in range(2):
                for rc in range(NRC):
                    u = rc % 2
                    s_evA[u].wait(ten)
                    s_sqA[u].wait(ten)
                    if tt == 0 and rc == 0:
                        s_ldx.wait(ten)
                        s_ldwqd.wait(ten)
                    for dc in range(NDC):
                        i = ten.matmul(
                            pqc[u][:],
                            wqd_sb[:, dc * QR + rc * 128: dc * QR + rc * 128 + 128],
                            xob[:, dc * TOK + tt * 512: dc * TOK + tt * 512 + 512],
                            start=(dc == 0), stop=(dc == NDC - 1))
                    s_mmA.inc(i)
                    s_mmA.wait(sca)
                    if tt == 1:
                        s_nrm.wait(sca)
                    s_evA[u].inc(sca.activation(qcr[:, rc * 512:(rc + 1) * 512],
                                                pqc[u][:], AF.Copy))
                    s_mmA.wait(sca)
                    s_sqA[u].wait(sca)
                    s_sqA[u].inc(sca.activation(sqb[u][:], pqc[u][:],
                                                AF.Square))
                    s_sqA[u].wait(ten)
                    if rc == 0:
                        s_stA.wait(ten)
                    i = ten.matmul(pst[:], ones_c[:], sqb[u][:],
                                   start=(rc == 0), stop=(rc == NRC - 1))
                    s_sqA[u].inc(i)
                s_sqA[(NRC - 1) % 2].wait(vec)
                s_stA.wait(vec)
                s_stA.inc(vec.tensor_scalar(st1[:], pst[:], 1.0 / QR, EPS_RMS,
                                            ALU.mult, ALU.add))
                s_stA.wait(sca)
                s_stA.inc(sca.activation(st2[:], st1[:], AF.Sqrt))
                s_stA.wait(vec)
                s_stA.inc(vec.reciprocal(st3[:], st2[:]))
                s_stA.wait(ten)
                s_nrm.wait(ten)
                s_stA.inc(ten.matmul(pbc[:], ones_r[:], st3[:], start=True,
                                     stop=True))
                s_stA.wait(vec)
                s_evA[0].wait(vec)
                s_evA[1].wait(vec)
                for rc in range(NRC):
                    s_nrm.inc(vec.tensor_tensor(qcr[:, rc * 512:(rc + 1) * 512],
                                                qcr[:, rc * 512:(rc + 1) * 512],
                                                pbc[:], ALU.mult))
                s_nrm.wait(vec)
                s_hw.wait(vec)
                for rc in range(NRC):
                    s_nrm.inc(vec.tensor_scalar(
                        qc_bf[:, rc * TOK + tt * 512: rc * TOK + tt * 512 + 512],
                        qcr[:, rc * 512:(rc + 1) * 512],
                        qnw_sb[:, rc:rc + 1], 0.0, ALU.mult, ALU.add))

            if debug:
                s_nrm.wait(gps)
                i = gps.dma_start(dbg["d_qc"], qc_bf[:])
                s_spill.inc(i, 16)

            s_nrm.wait(ten)
            s_evA[0].wait(ten)
            s_evA[1].wait(ten)
            s_sqA[0].wait(ten)
            s_sqA[1].wait(ten)
            for h in range(NH):
                sl = h % 3
                for tt in range(2):
                    u = tt % 2
                    s_stgn[u].wait(ten)
                    s_wu[sl].wait(ten)
                    for rc in range(NRC):
                        i = ten.matmul(
                            pqc[u][:], wu_sl[sl][:, rc * 128:(rc + 1) * 128],
                            qc_bf[:, rc * TOK + tt * 512: rc * TOK + tt * 512 + 512],
                            start=(rc == 0), stop=(rc == NRC - 1))
                    if tt == 1:
                        s_wuf[sl].inc(i)
                    else:
                        s_mmA.inc(i)
                    eng_sem = s_wuf[sl] if tt == 1 else s_mmA
                    eng_sem.wait(sca)
                    s_stgn[u].wait(sca)
                    s_stgn[u].inc(sca.activation(stg_n[u][:], pqc[u][:], AF.Copy))
                    s_stgn[u].wait(syn)
                    i = syn.dma_start(qspn[h, :, tt * 512:(tt + 1) * 512],
                                      stg_n[u][:])
                    s_stgn[u].inc(i, 16)
                if h + 3 < NH:
                    load_wu(h + 3)

            for h in range(NH):
                sl = h % 3
                for tt in range(2):
                    u = tt % 2
                    s_ppe[0].wait(ten)
                    s_wr[sl].wait(ten)
                    for g in range(2):
                        for rc in range(NRC):
                            i = ten.matmul(
                                ppe[:, g * 512:(g + 1) * 512],
                                wr_sl[sl][:, rc * DR + g * 32: rc * DR + g * 32 + 32],
                                qc_bf[:, rc * TOK + tt * 512: rc * TOK + tt * 512 + 512],
                                start=(rc == 0), stop=(rc == NRC - 1))
                    if tt == 1:
                        s_wrf[sl].inc(i)
                    else:
                        s_mmA.inc(i)
                    eng_sem = s_wrf[sl] if tt == 1 else s_mmA
                    eng_sem.wait(sca)
                    s_rope.wait(sca)
                    s_ppe[0].inc(sca.activation(qpe_f[u][:], ppe[:], AF.Copy))
                    s_ppe[0].wait(vec)
                    s_rope.wait(vec)
                    s_stgp[0].wait(vec)
                    s_stgp[1].wait(vec)
                    cq = cosq[:, tt * 512:(tt + 1) * 512]
                    sq = sinq[:, tt * 512:(tt + 1) * 512]
                    x1 = qpe_f[u][:, 0:512]
                    x2 = qpe_f[u][:, 512:1024]
                    s_rope.inc(vec.tensor_tensor(rtmp[:], x1, cq, ALU.mult))
                    s_rope.inc(vec.tensor_tensor(rtmp2[:], x2, sq, ALU.mult))
                    s_rope.wait(vec)
                    s_stgp[u].inc(vec.tensor_tensor(stg_p[u][:, 0:512], rtmp[:],
                                                    rtmp2[:], ALU.subtract))
                    s_stgp[u].wait(vec)
                    s_rope.inc(vec.tensor_tensor(rtmp[:], x1, sq, ALU.mult))
                    s_rope.inc(vec.tensor_tensor(rtmp2[:], x2, cq, ALU.mult))
                    s_rope.wait(vec)
                    s_stgp[u].inc(vec.tensor_tensor(stg_p[u][:, 512:1024],
                                                    rtmp[:], rtmp2[:], ALU.add))
                    s_stgp[u].wait(syn)
                    i = syn.dma_start(
                        qspp[h].rearrange("(g r) T -> r g T", g=2)[:, :, tt * 512:(tt + 1) * 512],
                        stg_p[u][:].rearrange("r (g t) -> r g t", g=2))
                    s_stgp[u].inc(i, 16)
                if h + 3 < NH:
                    load_wr(h + 3)

            for se in (s_stgn[0], s_stgn[1], s_stgp[0], s_stgp[1]):
                se.wait(syn)
            s_spill.wait(gps)
            barrier()

        stQ.close()

        stBC = ExitStack()
        kvc_bf = stBC.enter_context(nc.sbuf_tensor("kvc_bf", [128, NKC * S], BF16))
        kr_bf = stBC.enter_context(nc.sbuf_tensor("kr_bf", [DR, S], BF16))
        with ExitStack() as stB:
            def sbB(name, shape, dt):
                return stB.enter_context(nc.sbuf_tensor(name, shape, dt))
            cosk1 = sbB("cosk1", [32, 512], F32)
            sink1 = sbB("sink1", [32, 512], F32)
            poskb = sbB("poskb", [1, 512], F32)
            scrB = mk_tb_scratch(stB, "B")
            st1 = sbB("stB1", [1, 512], F32)
            st2 = sbB("stB2", [1, 512], F32)
            st3 = sbB("stB3", [1, 512], F32)
            xk = [sbB(f"xkB{i}", [128, NDC * 512], BF16) for i in range(2)]
            wkd_sb = sbB("wkd_sb", [128, NDC * KVR], BF16)
            wkr_sb = sbB("wkr_sb", [128, NDC * DR], BF16)
            kvcr = sbB("kvcr", [128, NKC * 512], F32)
            sqb = [sbB(f"sqB{i}", [128, 512], F32) for i in range(2)]
            krr = sbB("krr", [32, 1024], F32)
            krs = sbB("krs", [32, 1024], BF16)
            rtmp = sbB("rtmpB", [32, 512], F32)
            rtmp2 = sbB("rtmpB2", [32, 512], F32)
            psB = lambda n, s: stB.enter_context(nc.psum_tensor(n, s, F32))
            pkv = [psB("pkvB0", [128, 512]), psB("pkvB1", [128, 512])]
            pst = psB("pstB", [1, 512])
            pbc = psB("pbcB", [128, 512])
            pkr = psB("pkrB", [32, 1024])

            def kpos(c):
                s_tb.wait(gps)
                s_tb.inc(gps.iota(poskb[:], [[1, 512]], base=c * 512,
                                  channel_multiplier=0,
                                  allow_small_or_imprecise_dtypes=True))
                return poskb[:]

            s_ldw = pswd()
            s_xk = [pswd() for i in range(2)]
            s_xkf = [psem() for i in range(2)]
            s_mmB = psem()
            s_evB = [psem(), psem()]
            s_sqB = [psem(), psem()]
            s_stB = psem()
            s_nrmB = psem()
            s_krB = psem()
            s_ropeB = psem()
            s_krdma = psem()

            i = gps.dma_start(wkd_sb[:].rearrange("p (c r) -> p c r", c=NDC),
                              wkd.rearrange("(c p) r -> p c r", p=128))
            s_ldw.inc(i, 16)
            i = gps.dma_start(wkr_sb[:].rearrange("p (c r) -> p c r", c=NDC),
                              wkr.rearrange("(c p) r -> p c r", p=128))
            s_ldw.inc(i, 16)

            xf_r = xf.rearrange("(c p) t -> p c t", p=128)

            def load_xk(kb):
                sl = kb % 2
                s_xkf[sl].wait(gps)
                s_xk[sl].inc(
                    gps.dma_start(xk[sl][:].rearrange("p (c t) -> p c t", c=NDC),
                                  xf_r[:, :, kb * 512:(kb + 1) * 512]), 16)

            for kb in range(2):
                load_xk(kb)

            for kb in range(8):
                sl = kb % 2
                for rc in range(NKC):
                    u = rc % 2
                    s_evB[u].wait(ten)
                    s_sqB[u].wait(ten)
                    s_xk[sl].wait(ten)
                    if kb == 0 and rc == 0:
                        s_ldw.wait(ten)
                    for dc in range(NDC):
                        i = ten.matmul(
                            pkv[u][:],
                            wkd_sb[:, dc * KVR + rc * 128: dc * KVR + rc * 128 + 128],
                            xk[sl][:, dc * 512:(dc + 1) * 512],
                            start=(dc == 0), stop=(dc == NDC - 1))
                    s_mmB.inc(i)
                    s_mmB.wait(sca)
                    if kb > 0:
                        s_nrmB.wait(sca)
                    s_evB[u].inc(sca.activation(kvcr[:, rc * 512:(rc + 1) * 512],
                                                pkv[u][:], AF.Copy))
                    s_mmB.wait(sca)
                    s_sqB[u].wait(sca)
                    s_sqB[u].inc(sca.activation(sqb[u][:], pkv[u][:],
                                                AF.Square))
                    s_sqB[u].wait(ten)
                    if rc == 0:
                        s_stB.wait(ten)
                    i = ten.matmul(pst[:], ones_c[:], sqb[u][:],
                                   start=(rc == 0), stop=(rc == NKC - 1))
                    s_sqB[u].inc(i)
                s_krB.wait(ten)
                for g in range(2):
                    for dc in range(NDC):
                        i = ten.matmul(
                            pkr[:, g * 512:(g + 1) * 512],
                            wkr_sb[:, dc * DR + g * 32: dc * DR + g * 32 + 32],
                            xk[sl][:, dc * 512:(dc + 1) * 512],
                            start=(dc == 0), stop=(dc == NDC - 1))
                s_xkf[sl].inc(i)
                if kb + 2 < 8:
                    load_xk(kb + 2)
                s_sqB[(NKC - 1) % 2].wait(vec)
                s_stB.wait(vec)
                s_stB.inc(vec.tensor_scalar(st1[:], pst[:], 1.0 / KVR, EPS_RMS,
                                            ALU.mult, ALU.add))
                s_stB.wait(sca)
                s_stB.inc(sca.activation(st2[:], st1[:], AF.Sqrt))
                s_stB.wait(vec)
                s_stB.inc(vec.reciprocal(st3[:], st2[:]))
                s_stB.wait(ten)
                s_nrmB.wait(ten)
                s_stB.inc(ten.matmul(pbc[:], ones_r[:], st3[:], start=True,
                                     stop=True))
                s_stB.wait(vec)
                s_evB[0].wait(vec)
                s_evB[1].wait(vec)
                for rc in range(NKC):
                    s_nrmB.inc(vec.tensor_tensor(kvcr[:, rc * 512:(rc + 1) * 512],
                                                 kvcr[:, rc * 512:(rc + 1) * 512],
                                                 pbc[:], ALU.mult))
                s_nrmB.wait(vec)
                for rc in range(NKC):
                    s_nrmB.inc(vec.tensor_scalar(
                        kvc_bf[:, rc * S + kb * 512: rc * S + kb * 512 + 512],
                        kvcr[:, rc * 512:(rc + 1) * 512],
                        knw_sb[:, rc:rc + 1], 0.0, ALU.mult, ALU.add))
                s_ropeB.wait(vec)
                s_ropeB.wait(sca)
                build_tables_chunk(kpos(kb), cosk1[:], sink1[:], scrB)
                s_xkf[sl].wait(sca)
                s_ropeB.wait(sca)
                s_krB.inc(sca.activation(krr[:], pkr[:], AF.Copy))
                s_krB.wait(vec)
                s_tb.wait(vec)
                s_krdma.wait(vec)
                ck = cosk1[:]
                sk = sink1[:]
                x1 = krr[:, 0:512]
                x2 = krr[:, 512:1024]
                s_ropeB.inc(vec.tensor_tensor(rtmp[:], x1, ck, ALU.mult))
                s_ropeB.inc(vec.tensor_tensor(rtmp2[:], x2, sk, ALU.mult))
                s_ropeB.wait(vec)
                s_ropeB.inc(vec.tensor_tensor(krs[:, 0:512], rtmp[:], rtmp2[:],
                                              ALU.subtract))
                s_ropeB.wait(vec)
                s_ropeB.inc(vec.tensor_tensor(rtmp[:], x1, sk, ALU.mult))
                s_ropeB.inc(vec.tensor_tensor(rtmp2[:], x2, ck, ALU.mult))
                s_ropeB.wait(vec)
                s_ropeB.inc(vec.tensor_tensor(krs[:, 512:1024], rtmp[:],
                                              rtmp2[:], ALU.add))
                s_ropeB.wait(vec)
                s_ropeB.wait(syn)
                i = syn.dma_start(kr_bf[0:32, kb * 512:(kb + 1) * 512],
                                  krs[:, 0:512])
                s_krdma.inc(i, 16)
                i = syn.dma_start(kr_bf[32:64, kb * 512:(kb + 1) * 512],
                                  krs[:, 512:1024])
                s_krdma.inc(i, 16)

            s_krdma.wait(syn)
            if debug:
                s_nrmB.wait(gps)
                s_krdma.wait(gps)
                i = gps.dma_start(dbg["d_kvc"], kvc_bf[:])
                s_spill.inc(i, 16)
                i = gps.dma_start(dbg["d_kr"], kr_bf[:])
                s_spill.inc(i, 16)
                s_spill.wait(gps)
            barrier()
        pool_reset()

        stT.close()

        PTS = [0] * 10
        for t in range(1, 10):
            PTS[t] = PTS[t - 1] + 4 * (t - 1) * 128
        NPT = PTS[9]

        def tpp_val(gslot):
            return 16 * (gslot // 3 + 1)

        with ExitStack() as stC:
            def sbC(name, shape, dt):
                return stC.enter_context(nc.sbuf_tensor(name, shape, dt))
            K2 = sbC("K2", [128, 2 * S], BF16)
            V2 = [sbC(f"V2_{i}", [128, 2 * S], BF16) for i in range(2)]
            wkg = [sbC(f"wkgC{i}", [128, NKC * 512], BF16) for i in range(2)]
            S2 = [sbC("S2_0", [128, S], F32),
                  sbC("S2_1", [128, 7 * 512], F32)]
            P2 = [sbC(f"P2_{i}", [128, S], BF16) for i in range(3)]
            ostg = [sbC(f"ostg{i}", [128, TOK], BF16) for i in range(2)]
            Pt = sbC("Pt", [128, NPT], BF16)
            qn_sl = [sbC(f"qnC{i}", [128, TOK], BF16) for i in range(2)]
            qp_sl = [sbC(f"qpC{i}", [DR, TOK], BF16) for i in range(2)]
            smx = sbC("smx", [128, 8], F32)
            smn = sbC("smn", [128, 8], F32)
            ssum = sbC("ssum", [128, 8], F32)
            srcp = sbC("srcp", [128, 8], F32)
            psC = lambda n: stC.enter_context(
                nc.psum_tensor(n, [128, 1024], F32))
            pa = [psC("paC0"), psC("paC1")]
            pO = psC("pOC")

            s_qn = [pswd(), pswd()]
            s_qp = [pswd(), pswd()]
            s_wkg = [pswd(), pswd()]
            s_osp = [pswd(), pswd()]
            s_tpp = [sem(f"s_tp{i}") for i in range(3)]
            s_qfree = psem()
            s_wkfree = psem()
            s_mmC = psem()
            s_paC = [psem(), psem()]
            s_smax = psem()
            s_sexp = psem()
            s_sscl = psem()
            s_pvq = psem()
            s_oev = psem()
            PVQ0 = s_pvq.n

            wku_r = wku.rearrange("(c p) o -> p c o", p=128)

            def load_wkg(g):
                sl = g % 2
                s_wkfree.wait(gps)
                s_wkg[sl].inc(
                    gps.dma_start(wkg[sl][:].rearrange("p (c o) -> p c o",
                                                       c=NKC),
                                  wku_r[:, :, g * 512:(g + 1) * 512]), 16)

            def load_q(h):
                u = h % 2
                s_qfree.wait(gps)
                s_qn[u].inc(gps.dma_start(qn_sl[u][:], qspn[h]), 16)
                s_qp[u].inc(gps.dma_start(qp_sl[u][:], qspp[h]), 16)

            _rot = [0]

            def fill_bank():
                w = _rot[0] % 2
                _rot[0] += 1
                return w

            def emit_kbuild(g):
                sl = g % 2
                s_mmC.wait(ten)
                s_qfree.wait(ten)
                for hh in range(2):
                    for tq in range(4):
                        w = fill_bank()
                        s_paC[w].wait(ten)
                        if hh == 0 and tq == 0:
                            s_wkg[sl].wait(ten)
                        for rc in range(NKC):
                            for bk in range(2):
                                i = ten.matmul(
                                    pa[w][:, bk * 512:bk * 512 + 512],
                                    wkg[sl][:, rc * 512 + hh * 128:
                                            rc * 512 + hh * 128 + 128],
                                    kvc_bf[:, rc * S + tq * 1024 + bk * 512:
                                           rc * S + tq * 1024 + bk * 512 + 512],
                                    start=(rc == 0), stop=(rc == NKC - 1))
                        s_mmC.inc(i)
                        s_mmC.wait(sca)
                        i = sca.activation(
                            K2[:, hh * S + tq * 1024:
                               hh * S + tq * 1024 + 1024],
                            pa[w][:], AF.Copy)
                        s_paC[w].inc(i)

            def emit_vbuild(g):
                sl = g % 2
                vb = g % 2
                if g >= 2:
                    ten.wait_ge(s_pvq.h, PVQ0 + 8 * (2 * (g - 2) + 2))
                for f in range(8):
                    w = fill_bank()
                    s_paC[w].wait(ten)
                    if f == 0:
                        s_wkg[sl].wait(ten)
                    for kc in range(f * 4, f * 4 + 4):
                        for rc in range(NKC):
                            i = ten.matmul(
                                pa[w][:, (kc % 4) * 256:(kc % 4) * 256 + 256],
                                kvc_bf[:, rc * S + kc * 128:
                                       rc * S + kc * 128 + 128],
                                wkg[sl][:, rc * 512 + 256: rc * 512 + 512],
                                start=(rc == 0), stop=(rc == NKC - 1))
                    cov = s_wkfree if f == 7 else s_mmC
                    cov.inc(i)
                    cov.wait(vec)
                    i = vec.tensor_copy(V2[vb][:, f * 1024:(f + 1) * 1024],
                                        pa[w][:])
                    s_paC[w].inc(i)

            def emit_s(h, slots, stt):
                u = h % 2
                hh = h % 2
                smax_at, exp_at, scl_at, u2_of = stt

                def emit_exp(t):
                    c = t - 1
                    gs = 8 * h + (8 - t)
                    u2 = gs % 3
                    u2_of[t] = u2
                    sca.wait_ge(s_smax.h, smax_at[t])
                    if gs >= 3:
                        sca.wait_ge(s_tpp[gs % 3].h, tpp_val(gs - 3))
                    i = sca.activation(P2[u2][:, 0:t * 512],
                                       S2[t % 2][:, 0:t * 512], AF.Exp,
                                       bias=smn[:, c:c + 1], scale=SCALE,
                                       accum_out=ssum[:, c:c + 1])
                    s_sexp.inc(i)
                    exp_at[t] = s_sexp.n

                def emit_scale(t):
                    c = t - 1
                    vec.wait_ge(s_sexp.h, exp_at[t])
                    s_sscl.inc(vec.reciprocal(srcp[:, c:c + 1],
                                              ssum[:, c:c + 1]))
                    s_sscl.wait(vec)
                    s_sscl.inc(vec.tensor_scalar(
                        P2[u2_of[t]][:, 0:t * 512], P2[u2_of[t]][:, 0:t * 512],
                        srcp[:, c:c + 1], 0.0, ALU.mult, ALU.add))
                    scl_at[t] = s_sscl.n

                if slots[0] == 8:
                    s_paC[0].wait(ten)
                    s_paC[1].wait(ten)
                    s_qn[u].wait(ten)
                    s_qp[u].wait(ten)
                    s_sexp.wait(sca)
                    s_smax.wait(vec)
                    s_sscl.wait(sca)
                for t in slots:
                    qc0 = (t - 1) * 128
                    nun = (t + 1) // 2
                    SP = S2[t % 2]
                    for uu in range(nun):
                        width = min(1024, t * 512 - uu * 1024)
                        w = fill_bank()
                        s_paC[w].wait(ten)
                        for bk in range(width // 512):
                            o = uu * 1024 + bk * 512
                            ten.matmul(pa[w][:, bk * 512:bk * 512 + 512],
                                       qn_sl[u][:, qc0:qc0 + 128],
                                       K2[:, hh * S + o: hh * S + o + 512],
                                       start=True, stop=False)
                            i = ten.matmul(pa[w][:, bk * 512:bk * 512 + 512],
                                           qp_sl[u][:, qc0:qc0 + 128],
                                           kr_bf[:, o: o + 512],
                                           start=False, stop=True)
                        last = (t == 1)
                        cov = s_qfree if last else s_mmC
                        cov.inc(i)
                        base = uu * 1024
                        moff = width - 512 if uu == nun - 1 else width
                        if moff > 0:
                            cov.wait(sca)
                            if t <= 6:
                                sca.wait_ge(s_sexp.h, exp_at[t + 2])
                            i2 = sca.activation(SP[:, base:base + moff],
                                                pa[w][:, 0:moff], AF.Copy)
                            s_paC[w].inc(i2)
                        if uu == nun - 1:
                            cov.wait(vec)
                            if t <= 6:
                                vec.wait_ge(s_sexp.h, exp_at[t + 2])
                            i2 = vec.tensor_tensor(
                                SP[:, base + moff:base + width],
                                pa[w][:, moff:width], masksb[:], ALU.add)
                            s_paC[w].inc(i2)
                    c = t - 1
                    s_paC[0].wait(vec)
                    s_paC[1].wait(vec)
                    s_smax.inc(vec.reduce_max(smx[:, c:c + 1],
                                              SP[:, 0:t * 512], axis=AX))
                    s_smax.wait(vec)
                    s_smax.inc(vec.tensor_scalar(
                        smn[:, c:c + 1], smx[:, c:c + 1], -SCALE, 0.0,
                        ALU.mult, ALU.add))
                    smax_at[t] = s_smax.n
                    if t + 1 <= 8 and (t + 1) in smax_at:
                        emit_exp(t + 1)
                    if t + 2 <= 8 and (t + 2) in exp_at:
                        emit_scale(t + 2)
                if slots[-1] == 1:
                    emit_exp(1)
                    emit_scale(2)
                    emit_scale(1)

            def emit_softmax(h, scl_at, u2_of):
                for t in range(8, 0, -1):
                    gs = 8 * h + (8 - t)
                    syn.wait_ge(s_sscl.h, scl_at[t])
                    if h >= 1:
                        syn.wait_ge(s_pvq.h, PVQ0 + 8 * (h - 1) + t)
                    nb = 4 * t
                    dst = Pt[:, PTS[t]:PTS[t] + nb * 128].rearrange(
                        "k (b q) -> k b q", b=nb)
                    src_ = P2[u2_of[t]][:, 0:nb * 128].rearrange(
                        "q (b k) -> q b k", b=nb)
                    s_tpp[gs % 3].inc(
                        syn.dma_start(dst, src_, transpose=True), 16)

            def emit_pv(h):
                hh = h % 2
                vb = (h // 2) % 2
                s_paC[0].wait(ten)
                s_paC[1].wait(ten)
                s_oev.wait(ten)
                gl = 8 * h + 7
                for p in range(3):
                    ten.wait_ge(s_tpp[p].h, tpp_val(gl - ((gl - p) % 3)))
                for kc in range(32):
                    tmin = kc // 4 + 1
                    for t in range(tmin, 9):
                        i = ten.matmul(
                            pO[:, (t - 1) * 128: t * 128],
                            V2[vb][:, kc * 256 + hh * 128:
                                   kc * 256 + hh * 128 + 128],
                            Pt[:, PTS[t] + kc * 128: PTS[t] + kc * 128 + 128],
                            start=(kc == 0 and t in (1, 5)),
                            stop=(kc == 15 and t == 4) or (kc == 31 and t == 8))
                    if kc % 4 == 3:
                        s_pvq.inc(i)
                vec.wait_ge(s_pvq.h, PVQ0 + 8 * (h + 1))
                if h >= 2:
                    vec.wait_ge(s_osp[h % 2].h, s_osp[h % 2].n)
                s_oev.inc(vec.tensor_copy(ostg[h % 2][:], pO[:]))
                s_oev.wait(gps)
                s_osp[h % 2].inc(
                    gps.dma_start(osp[:, h * TOK:(h + 1) * TOK],
                                  ostg[h % 2][:]), 16)

            load_wkg(0)
            load_q(0)
            load_wkg(1)
            load_q(1)
            emit_kbuild(0)
            emit_vbuild(0)
            for h in range(NH):
                g = h // 2
                if h % 2 == 0 and g + 2 <= 7:
                    load_wkg(g + 2)
                stt = ({}, {}, {}, {})
                emit_s(h, [8, 7], stt)
                if h >= 1:
                    emit_pv(h - 1)
                emit_s(h, [6, 5, 4, 3, 2, 1], stt)
                emit_softmax(h, stt[2], stt[3])
                if h % 2 == 1 and g + 1 <= 7:
                    emit_kbuild(g + 1)
                    emit_vbuild(g + 1)
                if h + 2 < NH:
                    load_q(h + 2)
            emit_pv(NH - 1)
            barrier()
        pool_reset()

        stBC.close()

        xsum = sb("xsum", [128, NDC * TOK], F32)
        O_sb = sb("O_sb", [128, NH * TOK], BF16)
        st1 = sb("st1", [1, 512], F32)
        st2 = sb("st2", [1, 512], F32)
        st3 = sb("st3", [1, 512], F32)
        x1b = sb("x1b", [128, NDC * TOK], BF16)
        s_ln = sem("s_ln")

        def emit_ln(tag, to_x1b):
            stL = ExitStack()
            pstm = stL.enter_context(nc.psum_tensor(f"pstm{tag}", [1, 512], F32))
            pstv = stL.enter_context(nc.psum_tensor(f"pstv{tag}", [1, 512], F32))
            pbm = stL.enter_context(nc.psum_tensor(f"pbm{tag}", [128, 512], F32))
            pbv = stL.enter_context(nc.psum_tensor(f"pbv{tag}", [128, 512], F32))
            sqd = [stL.enter_context(
                nc.sbuf_tensor(f"sqD{tag}{i}", [128, 512], F32))
                for i in range(2)]
            s_sq = [sem(f"s_sqD{tag}{i}") for i in range(2)]
            s_stD = psem()
            for tt in range(2):
                for dc in range(NDC):
                    w = dc % 2
                    xsl = xsum[:, dc * TOK + tt * 512: dc * TOK + tt * 512 + 512]
                    s_sq[w].wait(sca)
                    s_ln.wait(sca)
                    s_sq[w].inc(sca.activation(sqd[w][:], xsl, AF.Square))
                    s_sq[w].wait(ten)
                    s_ln.wait(ten)
                    if dc == 0:
                        s_stD.wait(ten)
                    i = ten.matmul(pstv[:], ones_c[:], sqd[w][:],
                                   start=(dc == 0), stop=(dc == NDC - 1))
                    s_sq[w].inc(i)
                    i = ten.matmul(pstm[:], ones_c[:], xsl,
                                   start=(dc == 0), stop=(dc == NDC - 1))
                    if dc == NDC - 1:
                        s_stD.inc(i)
                    else:
                        s_sq[w].inc(i)
                s_stD.wait(vec)
                s_stD.inc(vec.tensor_scalar(st1[:], pstm[:], 1.0 / D, 0.0,
                                            ALU.mult, ALU.add))
                s_stD.inc(vec.tensor_scalar(st2[:], pstv[:], 1.0 / D, 0.0,
                                            ALU.mult, ALU.add))
                s_stD.wait(vec)
                s_stD.inc(vec.tensor_tensor(st3[:], st1[:], st1[:], ALU.mult))
                s_stD.wait(vec)
                s_stD.inc(vec.tensor_tensor(st2[:], st2[:], st3[:],
                                            ALU.subtract))
                s_stD.wait(vec)
                s_stD.inc(vec.tensor_scalar(st2[:], st2[:], 1.0, EPS_LN,
                                            ALU.mult, ALU.add))
                s_stD.wait(sca)
                s_stD.inc(sca.activation(st2[:], st2[:], AF.Sqrt))
                s_stD.wait(vec)
                s_stD.inc(vec.reciprocal(st2[:], st2[:]))
                s_stD.wait(ten)
                s_ln.wait(ten)
                i = ten.matmul(pbm[:], ones_r[:], st1[:], start=True, stop=True)
                s_stD.inc(i)
                i = ten.matmul(pbv[:], ones_r[:], st2[:], start=True, stop=True)
                s_stD.inc(i)
                s_stD.wait(vec)
                for dc in range(NDC):
                    xsl = xsum[:, dc * TOK + tt * 512: dc * TOK + tt * 512 + 512]
                    s_ln.inc(vec.tensor_tensor(xsl, xsl, pbm[:], ALU.subtract))
                s_ln.wait(vec)
                for dc in range(NDC):
                    xsl = xsum[:, dc * TOK + tt * 512: dc * TOK + tt * 512 + 512]
                    s_ln.inc(vec.tensor_tensor(xsl, xsl, pbv[:], ALU.mult))
                if to_x1b:
                    s_ln.wait(vec)
                    for dc in range(NDC):
                        s_ln.inc(vec.tensor_copy(
                            x1b[:, dc * TOK + tt * 512: dc * TOK + tt * 512 + 512],
                            xsum[:, dc * TOK + tt * 512: dc * TOK + tt * 512 + 512]))
            stL.close()

        with ExitStack() as stD:
            def sbD(name, shape, dt):
                return stD.enter_context(nc.sbuf_tensor(name, shape, dt))
            wo_sl = [sbD(f"woD{i}", [128, NDC * 128], BF16) for i in range(3)]
            s_wo = [pswd() for i in range(3)]
            s_wof = [psem() for i in range(3)]
            s_ldxs = psem()
            s_mmD = psem()
            s_evD = [psem(), psem()]
            pd = [stD.enter_context(nc.psum_tensor(f"pdD{i}", [128, 512], F32))
                  for i in range(2)]

            i = syn.dma_start(xsum[:].rearrange("p (c t) -> p c t", c=NDC),
                              xo.rearrange("(c p) t -> p c t", p=128))
            s_ldxs.inc(i, 16)
            s_ldo = pswd()
            s_osp[0].wait(gps)
            s_osp[1].wait(gps)
            s_ldo.inc(gps.dma_start(O_sb[:], osp), 16)
            if debug:
                s_ldo.wait(gps)
                i = gps.dma_start(dbg["d_obf"], O_sb[:])
                s_spill.inc(i, 16)
            wo_r = wo_.rearrange("(c p) o -> p c o", p=128)

            def load_wo(i_):
                sl = i_ % 3
                s_wof[sl].wait(gps)
                s_wo[sl].inc(
                    gps.dma_start(wo_sl[sl][:].rearrange("p (c o) -> p c o", c=NDC),
                                  wo_r[:, :, i_ * 128:(i_ + 1) * 128]), 16)

            for i_ in range(3):
                load_wo(i_)

            for oc in range(NDC):
                sl = oc % 3
                for tt in range(2):
                    w = tt % 2
                    s_evD[w].wait(ten)
                    s_wo[sl].wait(ten)
                    if oc == 0 and tt == 0:
                        s_ldo.wait(ten)
                    for ic in range(NDC):
                        i = ten.matmul(
                            pd[w][:], wo_sl[sl][:, ic * 128:(ic + 1) * 128],
                            O_sb[:, ic * TOK + tt * 512: ic * TOK + tt * 512 + 512],
                            start=(ic == 0), stop=(ic == NDC - 1))
                    if tt == 1:
                        s_wof[sl].inc(i)
                    else:
                        s_mmD.inc(i)
                    eng_sem = s_wof[sl] if tt == 1 else s_mmD
                    eng_sem.wait(vec)
                    if oc == 0 and tt == 0:
                        s_ldxs.wait(vec)
                    xsl = xsum[:, oc * TOK + tt * 512: oc * TOK + tt * 512 + 512]
                    s_evD[w].inc(vec.tensor_tensor(xsl, xsl, pd[w][:], ALU.add))
                if oc + 3 < NDC:
                    load_wo(oc + 3)
            s_evD[0].wait(vec)
            s_evD[1].wait(vec)
            vec_mark(s_ln)
            emit_ln("1", True)
            if debug:
                s_ln.wait(gps)
                i = gps.dma_start(dbg["d_x1"], xsum[:])
                s_spill.inc(i, 16)
                s_spill.wait(gps)
            barrier()
        pool_reset()

        with ExitStack() as stE:
            def sbE(name, shape, dt):
                return stE.enter_context(nc.sbuf_tensor(name, shape, dt))
            comb_bc = O_sb[:, 10240:10240 + TOK]
            h_buf = O_sb
            gs = [sbE(f"gsE{i}", [128, 512], BF16) for i in range(2)]
            us = [sbE(f"usE{i}", [128, 512], BF16) for i in range(2)]
            tmp_r = sbE("tmp_r", [128, 512], F32)
            wg_sl = [sbE(f"wgE{i}", [128, NDC * 128], BF16) for i in range(2)]
            wu_sl = [sbE(f"wuE{i}", [128, NDC * 128], BF16) for i in range(2)]
            RD = 8
            wd_sl = [sbE(f"wdE{i}", [128, D], BF16) for i in range(RD)]
            stRt = ExitStack()
            sbR = lambda n, s, d: stRt.enter_context(nc.sbuf_tensor(n, s, d))
            rw_sb = sbR("rw_sb", [128, NDC * 8], F32)
            lg = sbR("lg", [8, TOK], F32)
            lt = sbR("lt", [128, 64], F32)
            e1 = sbR("e1", [128, 8], F32)
            m1 = sbR("m1", [128, 1], F32)
            m2 = sbR("m2", [128, 1], F32)
            cc = sbR("cc", [128, 8], F32)
            crow = O_sb[0:1, 9216:9216 + TOK]
            stR = ExitStack()
            pr = stR.enter_context(nc.psum_tensor("prE", [8, 512], F32))
            pct = stR.enter_context(nc.psum_tensor("pctE", [1, 128], F32))
            pcb = stR.enter_context(nc.psum_tensor("pcbE", [128, 512], F32))
            ptr = stR.enter_context(nc.psum_tensor("ptrE", [128, 8], F32))

            s_ldr = psem()
            s_mmE = psem()
            s_rt = psem()
            s_cmb = psem()
            s_crw = psem()
            s_cbc = psem()
            s_wg = [pswd() for i in range(2)]
            s_wgf = [psem() for i in range(2)]
            s_wu = [pswd() for i in range(2)]
            s_wuf = [psem() for i in range(2)]
            s_wd = [pswd() for i in range(RD)]
            s_wdf = [psem() for i in range(RD)]
            s_gs = [psem(), psem()]
            s_us = [psem(), psem()]
            s_hw2 = psem()
            s_dn = [psem(), psem()]
            s_acc = psem()
            s_tmpr = psem()
            s_out = psem()

            i = syn.dma_start(rw_sb[:].rearrange("p (c o) -> p c o", c=NDC),
                              rw.rearrange("(c p) o -> p c o", p=128))
            s_ldr.inc(i, 16)
            for tt in range(2):
                s_ldr.wait(ten)
                s_rt.wait(ten)
                for dc in range(NDC):
                    i = ten.matmul(
                        pr[:], rw_sb[:, dc * 8:(dc + 1) * 8],
                        xsum[:, dc * TOK + tt * 512: dc * TOK + tt * 512 + 512],
                        start=(dc == 0), stop=(dc == NDC - 1))
                s_mmE.inc(i)
                s_mmE.wait(sca)
                s_rt.inc(sca.activation(lg[:, tt * 512:(tt + 1) * 512], pr[:],
                                        AF.Copy))
            for ch in range(8):
                s_rt.wait(ten)
                s_cmb.wait(ten)
                i = ten.transpose(ptr[:], lg[:, ch * 128:(ch + 1) * 128],
                                  ident_f[0:8, 0:8])
                s_rt.inc(i)
                s_rt.wait(vec)
                s_cmb.inc(vec.tensor_copy(lt[:, ch * 8:(ch + 1) * 8], ptr[:]))
                ltc = lt[:, ch * 8:(ch + 1) * 8]
                s_cmb.wait(vec)
                s_cmb.inc(vec.reduce_max(m1[:], ltc, axis=AX))
                s_cmb.wait(vec)
                s_cmb.inc(vec.tensor_scalar(e1[:], ltc, m1[:, 0:1], 1e30,
                                            ALU.is_ge, ALU.mult))
                s_cmb.wait(vec)
                s_cmb.inc(vec.tensor_tensor(e1[:], ltc, e1[:], ALU.subtract))
                s_cmb.wait(vec)
                s_cmb.inc(vec.reduce_max(m2[:], e1[:], axis=AX))
                s_cmb.wait(vec)
                s_cmb.inc(vec.tensor_tensor(m2[:], m2[:], m1[:], ALU.subtract))
                s_cmb.wait(sca)
                s_cmb.inc(sca.activation(m2[:], m2[:], AF.Exp))
                s_cmb.wait(vec)
                s_cmb.inc(vec.tensor_scalar(m1[:], m2[:], 1.0, 0.0, ALU.add,
                                            ALU.add))
                s_cmb.wait(vec)
                s_cmb.inc(vec.reciprocal(m1[:], m1[:]))
                s_cmb.wait(vec)
                s_cmb.inc(vec.tensor_tensor(m2[:], m2[:], m1[:], ALU.mult))
                s_cmb.wait(vec)
                s_cmb.inc(vec.tensor_tensor(cc[:, ch:ch + 1], m1[:], m2[:],
                                            ALU.add))
            for ch in range(8):
                s_cmb.wait(ten)
                s_crw.wait(ten)
                i = ten.transpose(pct[:], cc[:, ch:ch + 1], ident_f[:])
                s_crw.inc(i)
                s_crw.wait(vec)
                s_crw.inc(vec.tensor_copy(crow[:, ch * 128:(ch + 1) * 128],
                                          pct[:]))
            for tt in range(2):
                s_crw.wait(ten)
                s_cbc.wait(ten)
                i = ten.matmul(pcb[:], ones_br[:],
                               crow[:, tt * 512:(tt + 1) * 512], start=True,
                               stop=True)
                s_cbc.inc(i)
                s_cbc.wait(vec)
                s_cbc.inc(vec.tensor_copy(comb_bc[:, tt * 512:(tt + 1) * 512],
                                          pcb[:]))

            stR.close()
            stRt.close()
            stM = ExitStack()
            pg = [stM.enter_context(nc.psum_tensor(f"pgE{i}", [128, 512], F32))
                  for i in range(2)]
            pu = [stM.enter_context(nc.psum_tensor(f"puE{i}", [128, 512], F32))
                  for i in range(2)]
            pdn = [stM.enter_context(nc.psum_tensor(f"pdnE{i}", [128, 512], F32))
                   for i in range(2)]
            for se_ in (s_cbc, s_cmb, s_crw, s_rt):
                se_.wait(ten)

            sg_r = sg.rearrange("(c p) o -> p c o", p=128)
            su_r = su.rearrange("(c p) o -> p c o", p=128)
            rg_r = rg.rearrange("(c p) o -> p c o", p=128)
            ru_r = ru.rearrange("(c p) o -> p c o", p=128)

            def load_gu(gi):
                gsrc, usrc, ii = (sg_r, su_r, gi) if gi < 64 else (rg_r, ru_r,
                                                                   gi - 64)
                sl = gi % 2
                s_wgf[sl].wait(gps)
                s_wg[sl].inc(
                    gps.dma_start(wg_sl[sl][:].rearrange("p (c o) -> p c o", c=NDC),
                                  gsrc[:, :, ii * 128:(ii + 1) * 128]), 16)
                s_wuf[sl].wait(gps)
                s_wu[sl].inc(
                    gps.dma_start(wu_sl[sl][:].rearrange("p (c o) -> p c o", c=NDC),
                                  usrc[:, :, ii * 128:(ii + 1) * 128]), 16)

            def load_wd(gi):
                dsrc, ii = (sd, gi) if gi < 64 else (rd, gi - 64)
                sl = gi % RD
                s_wdf[sl].wait(gps)
                s_wd[sl].inc(
                    gps.dma_start(wd_sl[sl][:],
                                  dsrc[ii * 128:(ii + 1) * 128, :]), 16)

            for gi in range(2):
                load_gu(gi)
            for gi in range(RD):
                load_wd(gi)

            NGI = 72
            for gi in range(NGI):
                grp, icg = gi // 8, gi % 8
                sl = gi % 2
                for tt in range(2):
                    w = tt % 2
                    s_gs[w].wait(ten)
                    s_wg[sl].wait(ten)
                    for dc in range(NDC):
                        i = ten.matmul(
                            pg[w][:], wg_sl[sl][:, dc * 128:(dc + 1) * 128],
                            x1b[:, dc * TOK + tt * 512: dc * TOK + tt * 512 + 512],
                            start=(dc == 0), stop=(dc == NDC - 1))
                    (s_wgf[sl] if tt == 1 else s_mmE).inc(i)
                    s_us[w].wait(ten)
                    s_wu[sl].wait(ten)
                    for dc in range(NDC):
                        i = ten.matmul(
                            pu[w][:], wu_sl[sl][:, dc * 128:(dc + 1) * 128],
                            x1b[:, dc * TOK + tt * 512: dc * TOK + tt * 512 + 512],
                            start=(dc == 0), stop=(dc == NDC - 1))
                    (s_wuf[sl] if tt == 1 else s_mmE).inc(i)
                    (s_wgf[sl] if tt == 1 else s_mmE).wait(sca)
                    s_gs[w].wait(sca)
                    s_gs[w].inc(sca.activation(gs[w][:], pg[w][:], AF.Silu))
                    (s_wuf[sl] if tt == 1 else s_mmE).wait(vec)
                    s_us[w].wait(vec)
                    s_gs[w].wait(vec)
                    s_us[w].inc(vec.tensor_copy(us[w][:], pu[w][:]))
                    s_gs[w].wait(vec)
                    s_us[w].wait(vec)
                    if grp > 0:
                        s_wdf[(gi - 8) % RD].wait(vec)
                    s_gs[w].inc(vec.tensor_tensor(
                        h_buf[:, icg * TOK + tt * 512: icg * TOK + tt * 512 + 512],
                        gs[w][:], us[w][:], ALU.mult))
                if gi + 2 < NGI:
                    load_gu(gi + 2)
                if icg == 7:
                    routed = grp == 8
                    s_gs[0].wait(ten)
                    s_gs[1].wait(ten)
                    for oc in range(NDC):
                        for tt in range(2):
                            w = tt % 2
                            s_dn[w].wait(ten)
                            if oc == 0:
                                for ic2 in range(8):
                                    s_wd[(grp * 8 + ic2) % RD].wait(ten)
                            for ic2 in range(8):
                                gi2 = grp * 8 + ic2
                                dsl = gi2 % RD
                                i = ten.matmul(
                                    pdn[w][:], wd_sl[dsl][:, oc * 128:(oc + 1) * 128],
                                    h_buf[:, ic2 * TOK + tt * 512: ic2 * TOK + tt * 512 + 512],
                                    start=(ic2 == 0), stop=(ic2 == 7))
                                if oc == NDC - 1 and tt == 1:
                                    s_wdf[dsl].inc(i)
                            if not (oc == NDC - 1 and tt == 1):
                                s_mmE.inc(i)
                            cover = s_wdf[(grp * 8 + 7) % RD] if (
                                oc == NDC - 1 and tt == 1) else s_mmE
                            cover.wait(vec)
                            xsl = xsum[:, oc * TOK + tt * 512: oc * TOK + tt * 512 + 512]
                            if routed:
                                s_acc.wait(vec)
                                s_cbc.wait(vec)
                                s_dn[w].inc(vec.tensor_tensor(
                                    tmp_r[:], pdn[w][:],
                                    comb_bc[:, tt * 512:(tt + 1) * 512],
                                    ALU.mult))
                                s_dn[w].wait(vec)
                                s_acc.inc(vec.tensor_tensor(xsl, xsl, tmp_r[:],
                                                            ALU.add))
                            else:
                                s_dn[w].inc(vec.tensor_tensor(xsl, xsl,
                                                              pdn[w][:],
                                                              ALU.add))
                    for ic2 in range(8):
                        if grp * 8 + 8 + ic2 < NGI:
                            load_wd(grp * 8 + 8 + ic2)

            stM.close()
            s_acc.wait(vec)
            s_dn[0].wait(vec)
            s_dn[1].wait(vec)
            vec_mark(s_ln)
            emit_ln("2", False)
            s_ln.wait(syn)
            i = syn.dma_start(out_.rearrange("(c p) t -> p c t", p=128),
                              xsum[:].rearrange("p (c t) -> p c t", c=NDC))
            s_out.inc(i, 16)
            s_out.wait(syn)
    return nc


def _deint(nh):
    p = np.concatenate([np.arange(0, 64, 2), np.arange(1, 64, 2)])
    return np.concatenate([h * 64 + p for h in range(nh)])


def _wku_perm():
    idx = []
    for g in range(NH // 2):
        h0, h1 = 2 * g, 2 * g + 1
        idx += [np.arange(h0 * 256, h0 * 256 + 128),
                np.arange(h1 * 256, h1 * 256 + 128),
                np.arange(h0 * 256 + 128, h0 * 256 + 256),
                np.arange(h1 * 256 + 128, h1 * 256 + 256)]
    return np.concatenate(idx)


def kernel(**inp):
    x = np.ascontiguousarray(inp["L0_attn_x"], dtype=np.float32)
    T = lambda a: np.ascontiguousarray(np.asarray(a, dtype=np.float32).T)
    shared = {
        "wqd": T(inp["L0_attn_wq_down"]),
        "wqu": T(inp["L0_attn_wq_up"]),
        "wqr": np.ascontiguousarray(T(inp["L0_attn_wq_rope"])[:, _deint(NH)]),
        "qnw": np.ascontiguousarray(
            np.asarray(inp["L0_attn_q_norm_w"], np.float32).reshape(NRC, 128).T),
        "wkd": T(inp["L0_attn_wkv_down"]),
        "knw": np.ascontiguousarray(
            np.asarray(inp["L0_attn_kv_norm_w"], np.float32).reshape(NKC, 128).T),
        "wku": np.ascontiguousarray(T(inp["L0_attn_wkv_up"])[:, _wku_perm()]),
        "wkr": np.ascontiguousarray(T(inp["L0_attn_wk_rope"])[:, _deint(1)]),
        "wo": T(inp["L0_attn_wo"]),
        "rw": T(inp["L0_ffn_router_weight"]),
        "sg": T(inp["L0_ffn_sh_gate"]),
        "su": T(inp["L0_ffn_sh_up"]),
        "sd": T(inp["L0_ffn_sh_down"]),
        "rg": T(inp["L0_ffn_r_gate"]),
        "ru": T(inp["L0_ffn_r_up"]),
        "rd": T(inp["L0_ffn_r_down"]),
    }
    in_maps = []
    cols_l = []
    for core in range(8):
        b, j = core // 4, core % 4
        chunks = [4 * (t - 1) + j for t in range(1, 9)]
        cols = np.concatenate([np.arange(c * 128, c * 128 + 128) for c in chunks])
        cols_l.append((b, cols))
        xT = x[b].T
        m = dict(shared)
        m["xo"] = np.ascontiguousarray(xT[:, cols])
        m["xf"] = np.ascontiguousarray(xT)
        m["msk"] = np.where(
            np.arange(512)[None, :] <= 128 * j + np.arange(128)[:, None],
            np.float32(0), np.float32(-1e30)).astype(np.float32)
        m["pos"] = cols.astype(np.float32).reshape(1, TOK)
        in_maps.append(m)

    nc = build_nc()
    res = run_bass_kernel_spmd(nc, in_maps, core_ids=list(range(8)))
    full = np.zeros((2, S, D), dtype=np.float32)
    for core in range(8):
        b, cols = cols_l[core]
        full[b, cols, :] = np.asarray(res.results[core]["out"]).T
    return full


# version 09
# speedup vs baseline: 1.1508x; 1.0603x over previous
import numpy as np
import ml_dtypes
from contextlib import ExitStack

import concourse.bass as bass
import concourse.mybir as mybir
from concourse.bass_utils import run_bass_kernel_spmd

F32 = mybir.dt.float32
BF16 = mybir.dt.bfloat16
AX = mybir.AxisListType.X
AF = mybir.ActivationFunctionType
ALU = mybir.AluOpType

NH, DN, DR, DV = 16, 128, 64, 128
QR, KVR, D = 1536, 512, 2048
S = 4096
TOK = 1024
EPS_RMS, EPS_LN = 1e-6, 1e-5
SCALE = float(1.0 / np.sqrt(DN + DR))
PI = float(np.pi)
LN1E4 = float(np.log(10000.0))
NDC = D // 128
NRC = QR // 128
NKC = KVR // 128
POFF = [512 * t * (t - 1) // 2 for t in range(1, 10)]


class Sem:
    __slots__ = ("h", "n")

    def __init__(self, nc, st, name):
        self.h = st.enter_context(nc.semaphore(name))
        self.n = 0

    def inc(self, inst, k=1):
        inst.then_inc(self.h, k)
        self.n += k

    def wait(self, eng):
        if self.n:
            eng.wait_ge(self.h, self.n)


def build_nc(debug=False):
    nc = bass.Bass("TRN2", target_bir_lowering=False)
    ten, vec, sca, gps, syn = nc.tensor, nc.vector, nc.scalar, nc.gpsimd, nc.sync

    def din(name, shape):
        return nc.dram_tensor(name, shape, F32, kind="ExternalInput").ap()

    xo = din("xo", [D, TOK])
    xf = din("xf", [D, S])
    wqd = din("wqd", [D, QR])
    wqu = din("wqu", [QR, D])
    wqr = din("wqr", [QR, NH * DR])
    qnw = din("qnw", [128, NRC])
    wkd = din("wkd", [D, KVR])
    knw = din("knw", [128, NKC])
    wku = din("wku", [KVR, NH * 256])
    wkr = din("wkr", [D, DR])
    wo_ = din("wo", [D, D])
    rw = din("rw", [D, 8])
    sg = din("sg", [D, 8192])
    su = din("su", [D, 8192])
    sd = din("sd", [8192, D])
    rg = din("rg", [D, 1024])
    ru = din("ru", [D, 1024])
    rd = din("rd", [1024, D])
    msk = din("msk", [128, 512])
    pos = din("pos", [1, TOK])
    out_ = nc.dram_tensor("out", [D, TOK], F32, kind="ExternalOutput").ap()
    qspn = nc.dram_tensor("qspn", [NH, 128, TOK], BF16, kind="Internal").ap()
    qspp = nc.dram_tensor("qspp", [NH, DR, TOK], BF16, kind="Internal").ap()
    osp = nc.dram_tensor("osp", [128, NH * TOK], BF16, kind="Internal").ap()

    dbg = {}
    if debug:
        for nm, shp in [("d_qc", [128, NRC * TOK]), ("d_kvc", [128, NKC * S]),
                        ("d_kr", [DR, S]), ("d_obf", [128, NH * TOK]),
                        ("d_x1", [128, NDC * TOK])]:
            dbg[nm] = nc.dram_tensor(nm, shp, F32, kind="ExternalOutput").ap()

    st = ExitStack()
    with st:
        def sem(name):
            return Sem(nc, st, name)

        def sb(name, shape, dt):
            return st.enter_context(nc.sbuf_tensor(name, shape, dt))

        def ps(name, shape):
            return st.enter_context(nc.psum_tensor(name, shape, F32))

        ident_f = sb("ident_f", [128, 128], F32)
        iota_t = sb("iota_t", [128, 128], F32)
        ones_r = sb("ones_r", [1, 128], F32)
        ones_br = sb("ones_br", [1, 128], BF16)
        ones_c = sb("ones_c", [128, 1], F32)
        ones_cb = sb("ones_cb", [128, 1], BF16)
        masksb = sb("masksb", [128, 512], F32)
        ifq = sb("ifq", [32, 1], F32)
        qnw_sb = sb("qnw_sb", [128, NRC], F32)
        knw_sb = sb("knw_sb", [128, NKC], F32)
        negpi = sb("negpi", [128, 1], F32)
        bar_s = sb("bar_s", [1, 4], F32)

        s_init = sem("s_init")
        s_hw = sem("s_hw")
        s_bar = sem("s_bar")
        s_spill = sem("s_spill")

        def barrier():
            vec.tensor_copy(bar_s[:, 0:1], bar_s[:, 1:2]).then_inc(s_bar.h, 1)
            sca.activation(bar_s[:, 2:3], bar_s[:, 1:2],
                           AF.Copy).then_inc(s_bar.h, 1)
            for e in (ten, gps, syn):
                e.nop().then_inc(s_bar.h, 1)
            s_bar.n += 5
            for e in (ten, vec, sca, gps, syn):
                e.wait_ge(s_bar.h, s_bar.n)

        def vec_mark(sem_):
            sem_.inc(vec.tensor_copy(bar_s[:, 0:1], bar_s[:, 1:2]))

        _pool, _pool_i = {"c": [], "sw": []}, {"c": 0, "sw": 0}

        def psem(kind="c"):
            p = _pool[kind]
            if _pool_i[kind] >= len(p):
                p.append(Sem(nc, st, f"pool_{kind}{len(p)}"))
            s_ = p[_pool_i[kind]]
            _pool_i[kind] += 1
            return s_

        def pswd():
            return psem("sw")

        def pool_reset():
            _pool_i["c"] = 0
            _pool_i["sw"] = 0

        for dst, src in ((masksb, msk), (qnw_sb, qnw),
                         (knw_sb, knw)):
            i = syn.dma_start(dst[:], src)
            s_hw.inc(i, 16)
        s_init.inc(gps.iota(iota_t[:], [[1, 128]], channel_multiplier=-1,
                            allow_small_or_imprecise_dtypes=True))
        s_init.inc(gps.memset(ones_r[:], 1.0))
        s_init.inc(gps.memset(ones_br[:], 1.0))
        s_init.inc(gps.memset(ones_c[:], 1.0))
        s_init.inc(gps.memset(ones_cb[:], 1.0))
        s_init.inc(gps.memset(negpi[:], -PI))
        s_init.inc(gps.memset(bar_s[:], 0.0))
        s_init.wait(vec)
        s_init.wait(sca)
        s_init.inc(gps.iota(ifq[:], [[0, 1]], channel_multiplier=1,
                            allow_small_or_imprecise_dtypes=True))
        s_init.wait(vec)
        s_init.inc(vec.tensor_scalar(ident_f[:], iota_t[:], 0.0, 1.0,
                                     ALU.is_equal, ALU.mult))
        s_init.wait(sca)
        s_init.inc(sca.activation(ifq[:], ifq[:], AF.Exp, scale=-LN1E4 / 32.0))

        s_tb = sem("s_tb")
        stT = ExitStack()
        ptb = stT.enter_context(nc.psum_tensor("ptb", [128, 512], F32))
        stQ = ExitStack()
        cosq = stQ.enter_context(nc.sbuf_tensor("cosq", [128, TOK], F32))
        sinq = stQ.enter_context(nc.sbuf_tensor("sinq", [128, TOK], F32))
        ifq128 = stQ.enter_context(nc.sbuf_tensor("ifq128", [128, 1], F32))
        posq_sb = stQ.enter_context(nc.sbuf_tensor("posq_sb", [1, TOK], F32))

        def mk_tb_scratch(stx, tag, rows=32):
            return (stx.enter_context(
                        nc.sbuf_tensor(f"tbred{tag}", [rows, 512], F32)),
                    stx.enter_context(
                        nc.sbuf_tensor(f"tbo{tag}", [rows, 512], F32)),
                    stx.enter_context(
                        nc.sbuf_tensor(f"tbk{tag}", [rows, 512],
                                       mybir.dt.int32)),
                    stx.enter_context(
                        nc.sbuf_tensor(f"tbf{tag}", [rows, 512], F32)))

        def build_tables_chunk(prow, cos_dst, sin_dst, scr, ifq_ap=None,
                               rows=32):
            tbred, tbo, tbk, tbf = scr
            if ifq_ap is None:
                ifq_ap = ifq[:, 0:1]
            """cos/sin[i, p] = cos/sin(inv_freq[i] * pos[p]).
            Range-reduce ang - 2*pi*round(ang/2pi) via int cast + clamp
            (correct for either trunc or round-to-nearest cast), then Sin.
            dsts are [32, 512] APs."""
            s_init.wait(ten)
            s_tb.wait(ten)
            s_tb.inc(ten.matmul(ptb[0:rows, :], ones_r[:, 0:rows], prow,
                                start=True, stop=True))
            s_tb.wait(vec)
            s_tb.inc(vec.tensor_scalar(tbred[:], ptb[0:rows, :], ifq_ap, 0.0,
                                       ALU.mult, ALU.add))
            for dst, off in ((sin_dst, 0.0), (cos_dst, PI / 2)):
                s_tb.wait(vec)
                s_tb.inc(vec.tensor_scalar(tbo[:], tbred[:], off, 0.0,
                                           ALU.add, ALU.add))
                s_tb.wait(vec)
                s_tb.inc(vec.tensor_scalar(tbf[:], tbo[:], PI, 1.0 / (2 * PI),
                                           ALU.add, ALU.mult))
                s_tb.wait(vec)
                s_tb.inc(vec.tensor_copy(tbk[:], tbf[:]))
                s_tb.wait(vec)
                s_tb.inc(vec.tensor_copy(tbf[:], tbk[:]))
                s_tb.wait(vec)
                s_tb.inc(vec.tensor_scalar(tbf[:], tbf[:], 2 * PI, 0.0,
                                           ALU.mult, ALU.add))
                s_tb.wait(vec)
                s_tb.inc(vec.tensor_tensor(dst, tbo[:], tbf[:], ALU.subtract))
                s_tb.wait(vec)
                s_tb.inc(vec.tensor_scalar(tbf[:], dst, PI, 2 * PI, ALU.is_gt,
                                           ALU.mult))
                s_tb.wait(vec)
                s_tb.inc(vec.tensor_tensor(dst, dst, tbf[:], ALU.subtract))
                s_tb.wait(vec)
                s_tb.inc(vec.tensor_scalar(tbf[:], dst, -PI, 2 * PI, ALU.is_lt,
                                           ALU.mult))
                s_tb.wait(vec)
                s_tb.inc(vec.tensor_tensor(dst, dst, tbf[:], ALU.add))
                s_tb.wait(sca)
                sca.wait_ge(s_init.h, 5)
                s_tb.inc(sca.activation(dst, dst, AF.Sin))

        scrQ = mk_tb_scratch(stQ, "Q", rows=128)
        i = syn.dma_start(posq_sb[:], pos)
        s_hw.inc(i, 16)
        s_if = pswd()
        s_init.wait(gps)
        for g_ in range(4):
            i = gps.dma_start(ifq128[32 * g_:32 * g_ + 32, 0:1], ifq[:])
            s_if.inc(i, 16)
        s_if.wait(vec)
        s_hw.wait(vec)
        s_hw.wait(ten)
        for c in range(TOK // 512):
            build_tables_chunk(posq_sb[:, c * 512:(c + 1) * 512],
                               cosq[:, c * 512:(c + 1) * 512],
                               sinq[:, c * 512:(c + 1) * 512], scrQ,
                               ifq_ap=ifq128[:, 0:1], rows=128)

        with ExitStack() as stA:
            def sbA(name, shape, dt):
                return stA.enter_context(nc.sbuf_tensor(name, shape, dt))
            xob = sbA("xob", [128, NDC * TOK], BF16)
            wqd_sb = sbA("wqd_sb", [128, NDC * QR], BF16)
            st1 = sbA("stA1", [1, 512], F32)
            st2 = sbA("stA2", [1, 512], F32)
            st3 = sbA("stA3", [1, 512], F32)
            qcr = sbA("qcr", [128, NRC * 512], F32)
            qc_bf = sbA("qc_bf", [128, NRC * TOK], BF16)
            sqb = [sbA(f"sqA{i}", [128, 512], BF16) for i in range(2)]
            wu_sl = [sbA(f"wuA{i}", [128, NRC * 128], BF16) for i in range(3)]
            wr_sl = [sbA(f"wrA{i}", [128, NRC * 256], BF16) for i in range(2)]
            stg_n = [sbA(f"stgn{i}", [128, 512], BF16) for i in range(2)]
            qpe_e = sbA("qpe_e", [128, TOK], F32)
            qpe_o = sbA("qpe_o", [128, TOK], F32)
            rpe = sbA("rpeA", [128, TOK], F32)
            rpo = sbA("rpoA", [128, TOK], F32)
            rtmp = sbA("rtmpA", [128, 512], F32)
            rtmp2 = sbA("rtmpA2", [128, 512], F32)
            psA = lambda n, s: stA.enter_context(nc.psum_tensor(n, s, F32))
            pqc = [psA("pqcA0", [128, 512]), psA("pqcA1", [128, 512])]
            pst = psA("pstA", [1, 512])
            pbc = psA("pbcA", [128, 512])
            ppe_ = psA("ppeA0", [128, 1024])
            ppe = [ppe_, ppe_]

            s_ldx = pswd()
            s_ldwqd = pswd()
            s_wu = [pswd() for i in range(3)]
            s_wr = [pswd() for i in range(2)]
            s_wuf = [psem() for i in range(3)]
            s_wrf = [psem() for i in range(2)]
            s_mmA = psem()
            s_evA = [psem(), psem()]
            s_sqA = [psem(), psem()]
            s_stA = psem()
            s_nrm = psem()
            s_stgn = [psem(), psem()]
            s_ppe = [psem(), psem()]
            s_rope = psem()

            i = gps.dma_start(xob[:].rearrange("p (c t) -> p c t", c=NDC),
                              xo.rearrange("(c p) t -> p c t", p=128))
            s_ldx.inc(i, 16)
            i = gps.dma_start(wqd_sb[:].rearrange("p (c r) -> p c r", c=NDC),
                              wqd.rearrange("(c p) r -> p c r", p=128))
            s_ldwqd.inc(i, 16)

            wqu_r = wqu.rearrange("(c p) o -> p c o", p=128)
            wqr_r = wqr.rearrange("(c p) o -> p c o", p=128)

            def load_wu(i_):
                sl = i_ % 3
                s_wuf[sl].wait(gps)
                s_wu[sl].inc(
                    gps.dma_start(wu_sl[sl][:].rearrange("p (c o) -> p c o", c=NRC),
                                  wqu_r[:, :, i_ * 128:(i_ + 1) * 128]), 16)

            def load_wr(g4):
                sl = g4 % 2
                s_wrf[sl].wait(gps)
                s_wr[sl].inc(
                    gps.dma_start(
                        wr_sl[sl][:].rearrange("p (c o) -> p c o", c=NRC),
                        wqr_r[:, :, g4 * 256:(g4 + 1) * 256]), 16)

            for i_ in range(3):
                load_wu(i_)
            load_wr(0)
            load_wr(1)

            for tt in range(2):
                for rc in range(NRC):
                    u = rc % 2
                    s_evA[u].wait(ten)
                    s_sqA[u].wait(ten)
                    if tt == 0 and rc == 0:
                        s_ldx.wait(ten)
                        s_ldwqd.wait(ten)
                    for dc in range(NDC):
                        i = ten.matmul(
                            pqc[u][:],
                            wqd_sb[:, dc * QR + rc * 128: dc * QR + rc * 128 + 128],
                            xob[:, dc * TOK + tt * 512: dc * TOK + tt * 512 + 512],
                            start=(dc == 0), stop=(dc == NDC - 1))
                    s_mmA.inc(i)
                    s_mmA.wait(sca)
                    if tt == 1:
                        s_nrm.wait(sca)
                    s_evA[u].inc(sca.activation(qcr[:, rc * 512:(rc + 1) * 512],
                                                pqc[u][:], AF.Copy))
                    s_mmA.wait(sca)
                    s_sqA[u].wait(sca)
                    s_sqA[u].inc(sca.activation(sqb[u][:], pqc[u][:],
                                                AF.Square))
                    s_sqA[u].wait(ten)
                    if rc == 0:
                        s_stA.wait(ten)
                    i = ten.matmul(pst[:], ones_cb[:], sqb[u][:],
                                   start=(rc == 0), stop=(rc == NRC - 1))
                    s_sqA[u].inc(i)
                s_sqA[(NRC - 1) % 2].wait(vec)
                s_stA.wait(vec)
                s_stA.inc(vec.tensor_scalar(st1[:], pst[:], 1.0 / QR, EPS_RMS,
                                            ALU.mult, ALU.add))
                s_stA.wait(sca)
                s_stA.inc(sca.activation(st2[:], st1[:], AF.Sqrt))
                s_stA.wait(vec)
                s_stA.inc(vec.reciprocal(st3[:], st2[:]))
                s_stA.wait(ten)
                s_nrm.wait(ten)
                s_stA.inc(ten.matmul(pbc[:], ones_r[:], st3[:], start=True,
                                     stop=True))
                s_stA.wait(vec)
                s_evA[0].wait(vec)
                s_evA[1].wait(vec)
                for rc in range(NRC):
                    s_nrm.inc(vec.tensor_tensor(qcr[:, rc * 512:(rc + 1) * 512],
                                                qcr[:, rc * 512:(rc + 1) * 512],
                                                pbc[:], ALU.mult))
                s_nrm.wait(vec)
                s_hw.wait(vec)
                for rc in range(NRC):
                    s_nrm.inc(vec.tensor_scalar(
                        qc_bf[:, rc * TOK + tt * 512: rc * TOK + tt * 512 + 512],
                        qcr[:, rc * 512:(rc + 1) * 512],
                        qnw_sb[:, rc:rc + 1], 0.0, ALU.mult, ALU.add))

            if debug:
                s_nrm.wait(gps)
                i = gps.dma_start(dbg["d_qc"], qc_bf[:])
                s_spill.inc(i, 16)

            s_nrm.wait(ten)
            s_evA[0].wait(ten)
            s_evA[1].wait(ten)
            s_sqA[0].wait(ten)
            s_sqA[1].wait(ten)
            for h in range(NH):
                sl = h % 3
                for tt in range(2):
                    u = tt % 2
                    s_stgn[u].wait(ten)
                    s_wu[sl].wait(ten)
                    for rc in range(NRC):
                        i = ten.matmul(
                            pqc[u][:], wu_sl[sl][:, rc * 128:(rc + 1) * 128],
                            qc_bf[:, rc * TOK + tt * 512: rc * TOK + tt * 512 + 512],
                            start=(rc == 0), stop=(rc == NRC - 1))
                    if tt == 1:
                        s_wuf[sl].inc(i)
                    else:
                        s_mmA.inc(i)
                    eng_sem = s_wuf[sl] if tt == 1 else s_mmA
                    eng_sem.wait(sca)
                    s_stgn[u].wait(sca)
                    s_stgn[u].inc(sca.activation(stg_n[u][:], pqc[u][:], AF.Copy))
                    s_stgn[u].wait(syn)
                    i = syn.dma_start(qspn[h, :, tt * 512:(tt + 1) * 512],
                                      stg_n[u][:])
                    s_stgn[u].inc(i, 16)
                if h + 3 < NH:
                    load_wu(h + 3)

            s_spq = pswd()
            for g4 in range(4):
                sl = g4 % 2
                for half, dst in ((0, qpe_e), (1, qpe_o)):
                    u = 0
                    s_ppe[u].wait(ten)
                    s_wr[sl].wait(ten)
                    for rc in range(NRC):
                        for bk in range(2):
                            i = ten.matmul(
                                ppe[u][:, bk * 512:bk * 512 + 512],
                                wr_sl[sl][:, rc * 256 + half * 128:
                                          rc * 256 + half * 128 + 128],
                                qc_bf[:, rc * TOK + bk * 512:
                                      rc * TOK + bk * 512 + 512],
                                start=(rc == 0), stop=(rc == NRC - 1))
                    if half == 1:
                        s_wrf[sl].inc(i)
                    else:
                        s_mmA.inc(i)
                    eng_sem = s_wrf[sl] if half == 1 else s_mmA
                    eng_sem.wait(sca)
                    s_rope.wait(sca)
                    s_ppe[u].inc(sca.activation(dst[:], ppe[u][:], AF.Copy))
                if g4 + 2 < 4:
                    load_wr(g4 + 2)
                s_ppe[0].wait(vec)
                s_ppe[1].wait(vec)
                s_rope.wait(vec)
                s_spq.wait(vec)
                for bk in range(2):
                    sl_ = slice(bk * 512, bk * 512 + 512)
                    s_rope.inc(vec.tensor_tensor(rtmp[:], qpe_e[:, sl_],
                                                 cosq[:, sl_], ALU.mult))
                    s_rope.inc(vec.tensor_tensor(rtmp2[:], qpe_o[:, sl_],
                                                 sinq[:, sl_], ALU.mult))
                    s_rope.wait(vec)
                    s_rope.inc(vec.tensor_tensor(rpe[:, sl_], rtmp[:],
                                                 rtmp2[:], ALU.subtract))
                    s_rope.wait(vec)
                    s_rope.inc(vec.tensor_tensor(rtmp[:], qpe_e[:, sl_],
                                                 sinq[:, sl_], ALU.mult))
                    s_rope.inc(vec.tensor_tensor(rtmp2[:], qpe_o[:, sl_],
                                                 cosq[:, sl_], ALU.mult))
                    s_rope.wait(vec)
                    s_rope.inc(vec.tensor_tensor(rpo[:, sl_], rtmp[:],
                                                 rtmp2[:], ALU.add))
                    s_rope.wait(vec)
                s_rope.wait(gps)
                for hh in range(4):
                    h = 4 * g4 + hh
                    i = gps.dma_start(qspp[h][0:32, :],
                                      rpe[32 * hh:32 * hh + 32, :])
                    s_spq.inc(i, 16)
                    i = gps.dma_start(qspp[h][32:64, :],
                                      rpo[32 * hh:32 * hh + 32, :])
                    s_spq.inc(i, 16)

            for se in (s_stgn[0], s_stgn[1]):
                se.wait(syn)
            s_spq.wait(gps)
            s_spill.wait(gps)
            barrier()

        stQ.close()

        stBC = ExitStack()
        kvc_bf = stBC.enter_context(nc.sbuf_tensor("kvc_bf", [128, NKC * S], BF16))
        kr_bf = stBC.enter_context(nc.sbuf_tensor("kr_bf", [DR, S], BF16))
        with ExitStack() as stB:
            def sbB(name, shape, dt):
                return stB.enter_context(nc.sbuf_tensor(name, shape, dt))
            cosk_all = sbB("cosk_all", [32, S], F32)
            sink_all = sbB("sink_all", [32, S], F32)
            poskb = sbB("poskb", [1, 512], F32)
            scrB = mk_tb_scratch(stB, "B")
            st1 = sbB("stB1", [1, 512], F32)
            st2 = sbB("stB2", [1, 512], F32)
            st3 = sbB("stB3", [1, 512], F32)
            xk = [sbB(f"xkB{i}", [128, NDC * 512], BF16) for i in range(2)]
            wkd_sb = sbB("wkd_sb", [128, NDC * KVR], BF16)
            wkr_sb = sbB("wkr_sb", [128, NDC * DR], BF16)
            kvcr = sbB("kvcr", [128, NKC * 512], F32)
            sqb = [sbB(f"sqB{i}", [128, 512], BF16) for i in range(2)]
            krr = sbB("krr", [32, 1024], F32)
            krs = sbB("krs", [32, 1024], BF16)
            rtmp = sbB("rtmpB", [32, 512], F32)
            rtmp2 = sbB("rtmpB2", [32, 512], F32)
            psB = lambda n, s: stB.enter_context(nc.psum_tensor(n, s, F32))
            pkv = [psB("pkvB0", [128, 512]), psB("pkvB1", [128, 512])]
            pst = psB("pstB", [1, 512])
            pbc = psB("pbcB", [128, 512])
            pkr = psB("pkrB", [64, 512])

            def kpos(c):
                s_tb.wait(gps)
                s_tb.inc(gps.iota(poskb[:], [[1, 512]], base=c * 512,
                                  channel_multiplier=0,
                                  allow_small_or_imprecise_dtypes=True))
                return poskb[:]

            s_ldw = pswd()
            s_xk = [pswd() for i in range(2)]
            s_xkf = [psem() for i in range(2)]
            s_mmB = psem()
            s_evB = [psem(), psem()]
            s_sqB = [psem(), psem()]
            s_stB = psem()
            s_nrmB = psem()
            s_krB = psem()
            s_ropeB = psem()
            s_krdma = psem()

            i = gps.dma_start(wkd_sb[:].rearrange("p (c r) -> p c r", c=NDC),
                              wkd.rearrange("(c p) r -> p c r", p=128))
            s_ldw.inc(i, 16)
            i = gps.dma_start(wkr_sb[:].rearrange("p (c r) -> p c r", c=NDC),
                              wkr.rearrange("(c p) r -> p c r", p=128))
            s_ldw.inc(i, 16)

            xf_r = xf.rearrange("(c p) t -> p c t", p=128)

            def load_xk(kb):
                sl = kb % 2
                s_xkf[sl].wait(gps)
                s_xk[sl].inc(
                    gps.dma_start(xk[sl][:].rearrange("p (c t) -> p c t", c=NDC),
                                  xf_r[:, :, kb * 512:(kb + 1) * 512]), 16)

            for kb in range(2):
                load_xk(kb)

            for c_ in range(8):
                build_tables_chunk(kpos(c_),
                                   cosk_all[:, c_ * 512:(c_ + 1) * 512],
                                   sink_all[:, c_ * 512:(c_ + 1) * 512], scrB)

            for kb in range(8):
                sl = kb % 2
                for rc in range(NKC):
                    u = rc % 2
                    s_evB[u].wait(ten)
                    s_sqB[u].wait(ten)
                    s_xk[sl].wait(ten)
                    if kb == 0 and rc == 0:
                        s_ldw.wait(ten)
                    for dc in range(NDC):
                        i = ten.matmul(
                            pkv[u][:],
                            wkd_sb[:, dc * KVR + rc * 128: dc * KVR + rc * 128 + 128],
                            xk[sl][:, dc * 512:(dc + 1) * 512],
                            start=(dc == 0), stop=(dc == NDC - 1))
                    s_mmB.inc(i)
                    s_mmB.wait(sca)
                    if kb > 0:
                        s_nrmB.wait(sca)
                    s_evB[u].inc(sca.activation(kvcr[:, rc * 512:(rc + 1) * 512],
                                                pkv[u][:], AF.Copy))
                    s_mmB.wait(sca)
                    s_sqB[u].wait(sca)
                    s_sqB[u].inc(sca.activation(sqb[u][:], pkv[u][:],
                                                AF.Square))
                    s_sqB[u].wait(ten)
                    if rc == 0:
                        s_stB.wait(ten)
                    i = ten.matmul(pst[:], ones_cb[:], sqb[u][:],
                                   start=(rc == 0), stop=(rc == NKC - 1))
                    s_sqB[u].inc(i)
                s_krB.wait(ten)
                for dc in range(NDC):
                    i = ten.matmul(
                        pkr[:], wkr_sb[:, dc * DR: dc * DR + 64],
                        xk[sl][:, dc * 512:(dc + 1) * 512],
                        start=(dc == 0), stop=(dc == NDC - 1))
                s_xkf[sl].inc(i)
                if kb + 2 < 8:
                    load_xk(kb + 2)
                s_sqB[(NKC - 1) % 2].wait(vec)
                s_stB.wait(vec)
                s_stB.inc(vec.tensor_scalar(st1[:], pst[:], 1.0 / KVR, EPS_RMS,
                                            ALU.mult, ALU.add))
                s_stB.wait(sca)
                s_stB.inc(sca.activation(st2[:], st1[:], AF.Sqrt))
                s_stB.wait(vec)
                s_stB.inc(vec.reciprocal(st3[:], st2[:]))
                s_stB.wait(ten)
                s_nrmB.wait(ten)
                s_stB.inc(ten.matmul(pbc[:], ones_r[:], st3[:], start=True,
                                     stop=True))
                s_stB.wait(vec)
                s_evB[0].wait(vec)
                s_evB[1].wait(vec)
                for rc in range(NKC):
                    s_nrmB.inc(vec.tensor_tensor(kvcr[:, rc * 512:(rc + 1) * 512],
                                                 kvcr[:, rc * 512:(rc + 1) * 512],
                                                 pbc[:], ALU.mult))
                s_nrmB.wait(vec)
                for rc in range(NKC):
                    s_nrmB.inc(vec.tensor_scalar(
                        kvc_bf[:, rc * S + kb * 512: rc * S + kb * 512 + 512],
                        kvcr[:, rc * 512:(rc + 1) * 512],
                        knw_sb[:, rc:rc + 1], 0.0, ALU.mult, ALU.add))
                s_xkf[sl].wait(sca)
                s_ropeB.wait(sca)
                s_krB.inc(sca.activation(krr[:, 0:512], pkr[0:32, :],
                                         AF.Copy))
                s_krB.inc(sca.activation(krr[:, 512:1024], pkr[32:64, :],
                                         AF.Copy))
                s_krB.wait(vec)
                s_tb.wait(vec)
                s_krdma.wait(vec)
                ck = cosk_all[:, kb * 512:(kb + 1) * 512]
                sk = sink_all[:, kb * 512:(kb + 1) * 512]
                x1 = krr[:, 0:512]
                x2 = krr[:, 512:1024]
                s_ropeB.inc(vec.tensor_tensor(rtmp[:], x1, ck, ALU.mult))
                s_ropeB.inc(vec.tensor_tensor(rtmp2[:], x2, sk, ALU.mult))
                s_ropeB.wait(vec)
                s_ropeB.inc(vec.tensor_tensor(krs[:, 0:512], rtmp[:], rtmp2[:],
                                              ALU.subtract))
                s_ropeB.wait(vec)
                s_ropeB.inc(vec.tensor_tensor(rtmp[:], x1, sk, ALU.mult))
                s_ropeB.inc(vec.tensor_tensor(rtmp2[:], x2, ck, ALU.mult))
                s_ropeB.wait(vec)
                s_ropeB.inc(vec.tensor_tensor(krs[:, 512:1024], rtmp[:],
                                              rtmp2[:], ALU.add))
                s_ropeB.wait(vec)
                s_ropeB.wait(syn)
                i = syn.dma_start(kr_bf[0:32, kb * 512:(kb + 1) * 512],
                                  krs[:, 0:512])
                s_krdma.inc(i, 16)
                i = syn.dma_start(kr_bf[32:64, kb * 512:(kb + 1) * 512],
                                  krs[:, 512:1024])
                s_krdma.inc(i, 16)

            s_krdma.wait(syn)
            if debug:
                s_nrmB.wait(gps)
                s_krdma.wait(gps)
                i = gps.dma_start(dbg["d_kvc"], kvc_bf[:])
                s_spill.inc(i, 16)
                i = gps.dma_start(dbg["d_kr"], kr_bf[:])
                s_spill.inc(i, 16)
                s_spill.wait(gps)
            barrier()
        pool_reset()

        stT.close()

        PTS = [0] * 10
        for t in range(1, 10):
            PTS[t] = PTS[t - 1] + 4 * (t - 1) * 128
        NPT = PTS[9]

        def tpp_val(gslot):
            return 16 * (gslot // 3 + 1)

        with ExitStack() as stC:
            def sbC(name, shape, dt):
                return stC.enter_context(nc.sbuf_tensor(name, shape, dt))
            K2 = sbC("K2", [128, 2 * S], BF16)
            V2 = [sbC(f"V2_{i}", [128, 2 * S], BF16) for i in range(2)]
            wkg = [sbC(f"wkgC{i}", [128, NKC * 512], BF16) for i in range(2)]
            S2 = [sbC("S2_0", [128, S], F32),
                  sbC("S2_1", [128, 7 * 512], F32)]
            P2 = [sbC(f"P2_{i}", [128, S], BF16) for i in range(3)]
            ostg = [sbC(f"ostg{i}", [128, TOK], BF16) for i in range(2)]
            Pt = sbC("Pt", [128, NPT], BF16)
            qn_sl = [sbC(f"qnC{i}", [128, TOK], BF16) for i in range(2)]
            qp_sl = [sbC(f"qpC{i}", [DR, TOK], BF16) for i in range(2)]
            smx = sbC("smx", [128, 8], F32)
            smn = sbC("smn", [128, 8], F32)
            ssum = sbC("ssum", [128, 8], F32)
            srcp = sbC("srcp", [128, 8], F32)
            psC = lambda n: stC.enter_context(
                nc.psum_tensor(n, [128, 1024], F32))
            pa = [psC("paC0"), psC("paC1")]
            pO = psC("pOC")

            s_qn = [pswd(), pswd()]
            s_qp = [pswd(), pswd()]
            s_wkg = [pswd(), pswd()]
            s_osp = [pswd(), pswd()]
            s_tpp = [sem(f"s_tp{i}") for i in range(3)]
            s_qfree = psem()
            s_wkfree = psem()
            s_mmC = psem()
            s_paC = [psem(), psem()]
            s_smax = psem()
            s_sexp = psem()
            s_sscl = psem()
            s_pvq = psem()
            s_oev = psem()
            PVQ0 = s_pvq.n

            wku_r = wku.rearrange("(c p) o -> p c o", p=128)

            def load_wkg(g):
                sl = g % 2
                s_wkfree.wait(gps)
                s_wkg[sl].inc(
                    gps.dma_start(wkg[sl][:].rearrange("p (c o) -> p c o",
                                                       c=NKC),
                                  wku_r[:, :, g * 512:(g + 1) * 512]), 16)

            def load_q(h):
                u = h % 2
                s_qfree.wait(gps)
                s_qn[u].inc(gps.dma_start(qn_sl[u][:], qspn[h]), 16)
                s_qp[u].inc(gps.dma_start(qp_sl[u][:], qspp[h]), 16)

            _rot = [0]

            def fill_bank():
                w = _rot[0] % 2
                _rot[0] += 1
                return w

            def emit_kbuild(g):
                sl = g % 2
                s_mmC.wait(ten)
                s_qfree.wait(ten)
                for hh in range(2):
                    for tq in range(4):
                        w = fill_bank()
                        s_paC[w].wait(ten)
                        if hh == 0 and tq == 0:
                            s_wkg[sl].wait(ten)
                        for rc in range(NKC):
                            for bk in range(2):
                                i = ten.matmul(
                                    pa[w][:, bk * 512:bk * 512 + 512],
                                    wkg[sl][:, rc * 512 + hh * 128:
                                            rc * 512 + hh * 128 + 128],
                                    kvc_bf[:, rc * S + tq * 1024 + bk * 512:
                                           rc * S + tq * 1024 + bk * 512 + 512],
                                    start=(rc == 0), stop=(rc == NKC - 1))
                        s_mmC.inc(i)
                        s_mmC.wait(sca)
                        i = sca.activation(
                            K2[:, hh * S + tq * 1024:
                               hh * S + tq * 1024 + 1024],
                            pa[w][:], AF.Copy)
                        s_paC[w].inc(i)

            def emit_vbuild(g):
                sl = g % 2
                vb = g % 2
                if g >= 2:
                    ten.wait_ge(s_pvq.h, PVQ0 + 8 * (2 * (g - 2) + 2))
                for f in range(8):
                    w = fill_bank()
                    s_paC[w].wait(ten)
                    if f == 0:
                        s_wkg[sl].wait(ten)
                    for kc in range(f * 4, f * 4 + 4):
                        for rc in range(NKC):
                            i = ten.matmul(
                                pa[w][:, (kc % 4) * 256:(kc % 4) * 256 + 256],
                                kvc_bf[:, rc * S + kc * 128:
                                       rc * S + kc * 128 + 128],
                                wkg[sl][:, rc * 512 + 256: rc * 512 + 512],
                                start=(rc == 0), stop=(rc == NKC - 1))
                    cov = s_wkfree if f == 7 else s_mmC
                    cov.inc(i)
                    cov.wait(vec)
                    i = vec.tensor_copy(V2[vb][:, f * 1024:(f + 1) * 1024],
                                        pa[w][:])
                    s_paC[w].inc(i)

            def emit_s(h, slots, stt):
                u = h % 2
                hh = h % 2
                smax_at, exp_at, scl_at, u2_of = stt

                def emit_exp(t):
                    c = t - 1
                    gs = 8 * h + (8 - t)
                    u2 = gs % 3
                    u2_of[t] = u2
                    sca.wait_ge(s_smax.h, smax_at[t])
                    if gs >= 3:
                        sca.wait_ge(s_tpp[gs % 3].h, tpp_val(gs - 3))
                    i = sca.activation(P2[u2][:, 0:t * 512],
                                       S2[t % 2][:, 0:t * 512], AF.Exp,
                                       bias=smn[:, c:c + 1], scale=SCALE,
                                       accum_out=ssum[:, c:c + 1])
                    s_sexp.inc(i)
                    exp_at[t] = s_sexp.n

                def emit_scale(t):
                    c = t - 1
                    vec.wait_ge(s_sexp.h, exp_at[t])
                    s_sscl.inc(vec.reciprocal(srcp[:, c:c + 1],
                                              ssum[:, c:c + 1]))
                    s_sscl.wait(vec)
                    s_sscl.inc(vec.tensor_scalar(
                        P2[u2_of[t]][:, 0:t * 512], P2[u2_of[t]][:, 0:t * 512],
                        srcp[:, c:c + 1], 0.0, ALU.mult, ALU.add))
                    scl_at[t] = s_sscl.n

                if slots[0] == 8:
                    s_paC[0].wait(ten)
                    s_paC[1].wait(ten)
                    s_qn[u].wait(ten)
                    s_qp[u].wait(ten)
                    s_sexp.wait(sca)
                    s_smax.wait(vec)
                    s_sscl.wait(sca)
                for t in slots:
                    qc0 = (t - 1) * 128
                    nun = (t + 1) // 2
                    SP = S2[t % 2]
                    for uu in range(nun):
                        width = min(1024, t * 512 - uu * 1024)
                        w = fill_bank()
                        s_paC[w].wait(ten)
                        for bk in range(width // 512):
                            o = uu * 1024 + bk * 512
                            ten.matmul(pa[w][:, bk * 512:bk * 512 + 512],
                                       qn_sl[u][:, qc0:qc0 + 128],
                                       K2[:, hh * S + o: hh * S + o + 512],
                                       start=True, stop=False)
                            i = ten.matmul(pa[w][:, bk * 512:bk * 512 + 512],
                                           qp_sl[u][:, qc0:qc0 + 128],
                                           kr_bf[:, o: o + 512],
                                           start=False, stop=True)
                        last = (t == 1)
                        cov = s_qfree if last else s_mmC
                        cov.inc(i)
                        base = uu * 1024
                        moff = width - 512 if uu == nun - 1 else width
                        if moff > 0:
                            cov.wait(sca)
                            if t <= 6:
                                sca.wait_ge(s_sexp.h, exp_at[t + 2])
                            i2 = sca.activation(SP[:, base:base + moff],
                                                pa[w][:, 0:moff], AF.Copy)
                            s_paC[w].inc(i2)
                        if uu == nun - 1:
                            cov.wait(vec)
                            if t <= 6:
                                vec.wait_ge(s_sexp.h, exp_at[t + 2])
                            i2 = vec.tensor_tensor(
                                SP[:, base + moff:base + width],
                                pa[w][:, moff:width], masksb[:], ALU.add)
                            s_paC[w].inc(i2)
                    c = t - 1
                    s_paC[0].wait(vec)
                    s_paC[1].wait(vec)
                    s_smax.inc(vec.reduce_max(smx[:, c:c + 1],
                                              SP[:, 0:t * 512], axis=AX))
                    s_smax.wait(vec)
                    s_smax.inc(vec.tensor_scalar(
                        smn[:, c:c + 1], smx[:, c:c + 1], -SCALE, 0.0,
                        ALU.mult, ALU.add))
                    smax_at[t] = s_smax.n
                    if t + 1 <= 8 and (t + 1) in smax_at:
                        emit_exp(t + 1)
                    if t + 2 <= 8 and (t + 2) in exp_at:
                        emit_scale(t + 2)
                if slots[-1] == 1:
                    emit_exp(1)
                    emit_scale(2)
                    emit_scale(1)

            def emit_softmax(h, scl_at, u2_of):
                for t in range(8, 0, -1):
                    gs = 8 * h + (8 - t)
                    syn.wait_ge(s_sscl.h, scl_at[t])
                    if h >= 1:
                        syn.wait_ge(s_pvq.h, PVQ0 + 8 * (h - 1) + t)
                    nb = 4 * t
                    dst = Pt[:, PTS[t]:PTS[t] + nb * 128].rearrange(
                        "k (b q) -> k b q", b=nb)
                    src_ = P2[u2_of[t]][:, 0:nb * 128].rearrange(
                        "q (b k) -> q b k", b=nb)
                    s_tpp[gs % 3].inc(
                        syn.dma_start(dst, src_, transpose=True), 16)

            def emit_pv(h):
                hh = h % 2
                vb = (h // 2) % 2
                s_paC[0].wait(ten)
                s_paC[1].wait(ten)
                s_oev.wait(ten)
                gl = 8 * h + 7
                for p in range(3):
                    ten.wait_ge(s_tpp[p].h, tpp_val(gl - ((gl - p) % 3)))
                for kc in range(32):
                    tmin = kc // 4 + 1
                    for t in range(tmin, 9):
                        i = ten.matmul(
                            pO[:, (t - 1) * 128: t * 128],
                            V2[vb][:, kc * 256 + hh * 128:
                                   kc * 256 + hh * 128 + 128],
                            Pt[:, PTS[t] + kc * 128: PTS[t] + kc * 128 + 128],
                            start=(kc == 0 and t in (1, 5)),
                            stop=(kc == 15 and t == 4) or (kc == 31 and t == 8))
                    if kc % 4 == 3:
                        s_pvq.inc(i)
                vec.wait_ge(s_pvq.h, PVQ0 + 8 * (h + 1))
                if h >= 2:
                    vec.wait_ge(s_osp[h % 2].h, s_osp[h % 2].n)
                s_oev.inc(vec.tensor_copy(ostg[h % 2][:], pO[:]))
                s_oev.wait(gps)
                s_osp[h % 2].inc(
                    gps.dma_start(osp[:, h * TOK:(h + 1) * TOK],
                                  ostg[h % 2][:]), 16)

            load_wkg(0)
            load_q(0)
            load_wkg(1)
            load_q(1)
            emit_kbuild(0)
            emit_vbuild(0)
            for h in range(NH):
                g = h // 2
                if h % 2 == 0 and g + 2 <= 7:
                    load_wkg(g + 2)
                stt = ({}, {}, {}, {})
                emit_s(h, [8, 7], stt)
                if h >= 1:
                    emit_pv(h - 1)
                emit_s(h, [6, 5, 4, 3, 2, 1], stt)
                emit_softmax(h, stt[2], stt[3])
                if h % 2 == 1 and g + 1 <= 7:
                    emit_kbuild(g + 1)
                    emit_vbuild(g + 1)
                if h + 2 < NH:
                    load_q(h + 2)
            emit_pv(NH - 1)
            barrier()
        pool_reset()

        stBC.close()

        xsum = sb("xsum", [128, NDC * TOK], F32)
        O_sb = sb("O_sb", [128, NH * TOK], BF16)
        st1 = sb("st1", [1, 512], F32)
        st2 = sb("st2", [1, 512], F32)
        st3 = sb("st3", [1, 512], F32)
        x1b = sb("x1b", [128, NDC * TOK], BF16)
        s_ln = sem("s_ln")

        def emit_ln(tag, to_x1b):
            stL = ExitStack()
            pstm = stL.enter_context(nc.psum_tensor(f"pstm{tag}", [1, 512], F32))
            pstv = stL.enter_context(nc.psum_tensor(f"pstv{tag}", [1, 512], F32))
            pbm = stL.enter_context(nc.psum_tensor(f"pbm{tag}", [128, 512], F32))
            pbv = stL.enter_context(nc.psum_tensor(f"pbv{tag}", [128, 512], F32))
            sqd = [stL.enter_context(
                nc.sbuf_tensor(f"sqD{tag}{i}", [128, 512], F32))
                for i in range(2)]
            s_sq = [sem(f"s_sqD{tag}{i}") for i in range(2)]
            s_stD = psem()
            for tt in range(2):
                for dc in range(NDC):
                    w = dc % 2
                    xsl = xsum[:, dc * TOK + tt * 512: dc * TOK + tt * 512 + 512]
                    s_sq[w].wait(sca)
                    s_ln.wait(sca)
                    s_sq[w].inc(sca.activation(sqd[w][:], xsl, AF.Square))
                    s_sq[w].wait(ten)
                    s_ln.wait(ten)
                    if dc == 0:
                        s_stD.wait(ten)
                    i = ten.matmul(pstv[:], ones_c[:], sqd[w][:],
                                   start=(dc == 0), stop=(dc == NDC - 1))
                    s_sq[w].inc(i)
                    i = ten.matmul(pstm[:], ones_c[:], xsl,
                                   start=(dc == 0), stop=(dc == NDC - 1))
                    if dc == NDC - 1:
                        s_stD.inc(i)
                    else:
                        s_sq[w].inc(i)
                s_stD.wait(vec)
                s_stD.inc(vec.tensor_scalar(st1[:], pstm[:], 1.0 / D, 0.0,
                                            ALU.mult, ALU.add))
                s_stD.inc(vec.tensor_scalar(st2[:], pstv[:], 1.0 / D, 0.0,
                                            ALU.mult, ALU.add))
                s_stD.wait(vec)
                s_stD.inc(vec.tensor_tensor(st3[:], st1[:], st1[:], ALU.mult))
                s_stD.wait(vec)
                s_stD.inc(vec.tensor_tensor(st2[:], st2[:], st3[:],
                                            ALU.subtract))
                s_stD.wait(vec)
                s_stD.inc(vec.tensor_scalar(st2[:], st2[:], 1.0, EPS_LN,
                                            ALU.mult, ALU.add))
                s_stD.wait(sca)
                s_stD.inc(sca.activation(st2[:], st2[:], AF.Sqrt))
                s_stD.wait(vec)
                s_stD.inc(vec.reciprocal(st2[:], st2[:]))
                s_stD.wait(ten)
                s_ln.wait(ten)
                i = ten.matmul(pbm[:], ones_r[:], st1[:], start=True, stop=True)
                s_stD.inc(i)
                i = ten.matmul(pbv[:], ones_r[:], st2[:], start=True, stop=True)
                s_stD.inc(i)
                s_stD.wait(vec)
                for dc in range(NDC):
                    xsl = xsum[:, dc * TOK + tt * 512: dc * TOK + tt * 512 + 512]
                    s_ln.inc(vec.tensor_tensor(xsl, xsl, pbm[:], ALU.subtract))
                s_ln.wait(vec)
                for dc in range(NDC):
                    xsl = xsum[:, dc * TOK + tt * 512: dc * TOK + tt * 512 + 512]
                    s_ln.inc(vec.tensor_tensor(xsl, xsl, pbv[:], ALU.mult))
                if to_x1b:
                    s_ln.wait(vec)
                    for dc in range(NDC):
                        s_ln.inc(vec.tensor_copy(
                            x1b[:, dc * TOK + tt * 512: dc * TOK + tt * 512 + 512],
                            xsum[:, dc * TOK + tt * 512: dc * TOK + tt * 512 + 512]))
            stL.close()

        with ExitStack() as stD:
            def sbD(name, shape, dt):
                return stD.enter_context(nc.sbuf_tensor(name, shape, dt))
            wo_sl = [sbD(f"woD{i}", [128, NDC * 128], BF16) for i in range(3)]
            s_wo = [pswd() for i in range(3)]
            s_wof = [psem() for i in range(3)]
            s_ldxs = psem()
            s_mmD = psem()
            s_evD = [psem(), psem()]
            pd = [stD.enter_context(nc.psum_tensor(f"pdD{i}", [128, 512], F32))
                  for i in range(2)]

            i = syn.dma_start(xsum[:].rearrange("p (c t) -> p c t", c=NDC),
                              xo.rearrange("(c p) t -> p c t", p=128))
            s_ldxs.inc(i, 16)
            s_ldo = pswd()
            s_osp[0].wait(gps)
            s_osp[1].wait(gps)
            s_ldo.inc(gps.dma_start(O_sb[:], osp), 16)
            if debug:
                s_ldo.wait(gps)
                i = gps.dma_start(dbg["d_obf"], O_sb[:])
                s_spill.inc(i, 16)
            wo_r = wo_.rearrange("(c p) o -> p c o", p=128)

            def load_wo(i_):
                sl = i_ % 3
                s_wof[sl].wait(gps)
                s_wo[sl].inc(
                    gps.dma_start(wo_sl[sl][:].rearrange("p (c o) -> p c o", c=NDC),
                                  wo_r[:, :, i_ * 128:(i_ + 1) * 128]), 16)

            for i_ in range(3):
                load_wo(i_)

            for oc in range(NDC):
                sl = oc % 3
                for tt in range(2):
                    w = tt % 2
                    s_evD[w].wait(ten)
                    s_wo[sl].wait(ten)
                    if oc == 0 and tt == 0:
                        s_ldo.wait(ten)
                    for ic in range(NDC):
                        i = ten.matmul(
                            pd[w][:], wo_sl[sl][:, ic * 128:(ic + 1) * 128],
                            O_sb[:, ic * TOK + tt * 512: ic * TOK + tt * 512 + 512],
                            start=(ic == 0), stop=(ic == NDC - 1))
                    if tt == 1:
                        s_wof[sl].inc(i)
                    else:
                        s_mmD.inc(i)
                    eng_sem = s_wof[sl] if tt == 1 else s_mmD
                    eng_sem.wait(vec)
                    if oc == 0 and tt == 0:
                        s_ldxs.wait(vec)
                    xsl = xsum[:, oc * TOK + tt * 512: oc * TOK + tt * 512 + 512]
                    s_evD[w].inc(vec.tensor_tensor(xsl, xsl, pd[w][:], ALU.add))
                if oc + 3 < NDC:
                    load_wo(oc + 3)
            s_evD[0].wait(vec)
            s_evD[1].wait(vec)
            vec_mark(s_ln)
            emit_ln("1", True)
            if debug:
                s_ln.wait(gps)
                i = gps.dma_start(dbg["d_x1"], xsum[:])
                s_spill.inc(i, 16)
                s_spill.wait(gps)
            barrier()
        pool_reset()

        with ExitStack() as stE:
            def sbE(name, shape, dt):
                return stE.enter_context(nc.sbuf_tensor(name, shape, dt))
            comb_bc = O_sb[:, 10240:10240 + TOK]
            h_buf = O_sb
            gs = [sbE(f"gsE{i}", [128, 512], BF16) for i in range(2)]
            us = [sbE(f"usE{i}", [128, 512], BF16) for i in range(2)]
            tmp_r = sbE("tmp_r", [128, 512], F32)
            wg_sl = [sbE(f"wgE{i}", [128, NDC * 128], BF16) for i in range(2)]
            wu_sl = [sbE(f"wuE{i}", [128, NDC * 128], BF16) for i in range(2)]
            RD = 8
            wd_sl = [sbE(f"wdE{i}", [128, D], BF16) for i in range(RD)]
            stRt = ExitStack()
            sbR = lambda n, s, d: stRt.enter_context(nc.sbuf_tensor(n, s, d))
            rw_sb = sbR("rw_sb", [128, NDC * 8], F32)
            lg = sbR("lg", [8, TOK], F32)
            lt = sbR("lt", [128, 64], F32)
            e1 = sbR("e1", [128, 8], F32)
            m1 = sbR("m1", [128, 1], F32)
            m2 = sbR("m2", [128, 1], F32)
            cc = sbR("cc", [128, 8], F32)
            crow = O_sb[0:1, 9216:9216 + TOK]
            stR = ExitStack()
            pr = stR.enter_context(nc.psum_tensor("prE", [8, 512], F32))
            pct = stR.enter_context(nc.psum_tensor("pctE", [1, 128], F32))
            pcb = stR.enter_context(nc.psum_tensor("pcbE", [128, 512], F32))
            ptr = stR.enter_context(nc.psum_tensor("ptrE", [128, 8], F32))

            s_ldr = psem()
            s_mmE = psem()
            s_rt = psem()
            s_cmb = psem()
            s_crw = psem()
            s_cbc = psem()
            s_wg = [pswd() for i in range(2)]
            s_wgf = [psem() for i in range(2)]
            s_wu = [pswd() for i in range(2)]
            s_wuf = [psem() for i in range(2)]
            s_wd = [pswd() for i in range(RD)]
            s_wdf = [psem() for i in range(RD)]
            s_gs = [psem(), psem()]
            s_us = [psem(), psem()]
            s_hw2 = psem()
            s_dn = [psem(), psem()]
            s_acc = psem()
            s_tmpr = psem()
            s_out = psem()

            i = syn.dma_start(rw_sb[:].rearrange("p (c o) -> p c o", c=NDC),
                              rw.rearrange("(c p) o -> p c o", p=128))
            s_ldr.inc(i, 16)
            for tt in range(2):
                s_ldr.wait(ten)
                s_rt.wait(ten)
                for dc in range(NDC):
                    i = ten.matmul(
                        pr[:], rw_sb[:, dc * 8:(dc + 1) * 8],
                        xsum[:, dc * TOK + tt * 512: dc * TOK + tt * 512 + 512],
                        start=(dc == 0), stop=(dc == NDC - 1))
                s_mmE.inc(i)
                s_mmE.wait(sca)
                s_rt.inc(sca.activation(lg[:, tt * 512:(tt + 1) * 512], pr[:],
                                        AF.Copy))
            for ch in range(8):
                s_rt.wait(ten)
                s_cmb.wait(ten)
                i = ten.transpose(ptr[:], lg[:, ch * 128:(ch + 1) * 128],
                                  ident_f[0:8, 0:8])
                s_rt.inc(i)
                s_rt.wait(vec)
                s_cmb.inc(vec.tensor_copy(lt[:, ch * 8:(ch + 1) * 8], ptr[:]))
                ltc = lt[:, ch * 8:(ch + 1) * 8]
                s_cmb.wait(vec)
                s_cmb.inc(vec.reduce_max(m1[:], ltc, axis=AX))
                s_cmb.wait(vec)
                s_cmb.inc(vec.tensor_scalar(e1[:], ltc, m1[:, 0:1], 1e30,
                                            ALU.is_ge, ALU.mult))
                s_cmb.wait(vec)
                s_cmb.inc(vec.tensor_tensor(e1[:], ltc, e1[:], ALU.subtract))
                s_cmb.wait(vec)
                s_cmb.inc(vec.reduce_max(m2[:], e1[:], axis=AX))
                s_cmb.wait(vec)
                s_cmb.inc(vec.tensor_tensor(m2[:], m2[:], m1[:], ALU.subtract))
                s_cmb.wait(sca)
                s_cmb.inc(sca.activation(m2[:], m2[:], AF.Exp))
                s_cmb.wait(vec)
                s_cmb.inc(vec.tensor_scalar(m1[:], m2[:], 1.0, 0.0, ALU.add,
                                            ALU.add))
                s_cmb.wait(vec)
                s_cmb.inc(vec.reciprocal(m1[:], m1[:]))
                s_cmb.wait(vec)
                s_cmb.inc(vec.tensor_tensor(m2[:], m2[:], m1[:], ALU.mult))
                s_cmb.wait(vec)
                s_cmb.inc(vec.tensor_tensor(cc[:, ch:ch + 1], m1[:], m2[:],
                                            ALU.add))
            for ch in range(8):
                s_cmb.wait(ten)
                s_crw.wait(ten)
                i = ten.transpose(pct[:], cc[:, ch:ch + 1], ident_f[:])
                s_crw.inc(i)
                s_crw.wait(vec)
                s_crw.inc(vec.tensor_copy(crow[:, ch * 128:(ch + 1) * 128],
                                          pct[:]))
            for tt in range(2):
                s_crw.wait(ten)
                s_cbc.wait(ten)
                i = ten.matmul(pcb[:], ones_br[:],
                               crow[:, tt * 512:(tt + 1) * 512], start=True,
                               stop=True)
                s_cbc.inc(i)
                s_cbc.wait(vec)
                s_cbc.inc(vec.tensor_copy(comb_bc[:, tt * 512:(tt + 1) * 512],
                                          pcb[:]))

            stR.close()
            stRt.close()
            stM = ExitStack()
            pg = [stM.enter_context(nc.psum_tensor(f"pgE{i}", [128, 512], F32))
                  for i in range(2)]
            pu = [stM.enter_context(nc.psum_tensor(f"puE{i}", [128, 512], F32))
                  for i in range(2)]
            pdn = [stM.enter_context(nc.psum_tensor(f"pdnE{i}", [128, 512], F32))
                   for i in range(2)]
            for se_ in (s_cbc, s_cmb, s_crw, s_rt):
                se_.wait(ten)

            sg_r = sg.rearrange("(c p) o -> p c o", p=128)
            su_r = su.rearrange("(c p) o -> p c o", p=128)
            rg_r = rg.rearrange("(c p) o -> p c o", p=128)
            ru_r = ru.rearrange("(c p) o -> p c o", p=128)

            def load_gu(gi):
                gsrc, usrc, ii = (sg_r, su_r, gi) if gi < 64 else (rg_r, ru_r,
                                                                   gi - 64)
                sl = gi % 2
                s_wgf[sl].wait(gps)
                s_wg[sl].inc(
                    gps.dma_start(wg_sl[sl][:].rearrange("p (c o) -> p c o", c=NDC),
                                  gsrc[:, :, ii * 128:(ii + 1) * 128]), 16)
                s_wuf[sl].wait(gps)
                s_wu[sl].inc(
                    gps.dma_start(wu_sl[sl][:].rearrange("p (c o) -> p c o", c=NDC),
                                  usrc[:, :, ii * 128:(ii + 1) * 128]), 16)

            def load_wd(gi):
                dsrc, ii = (sd, gi) if gi < 64 else (rd, gi - 64)
                sl = gi % RD
                s_wdf[sl].wait(gps)
                s_wd[sl].inc(
                    gps.dma_start(wd_sl[sl][:],
                                  dsrc[ii * 128:(ii + 1) * 128, :]), 16)

            for gi in range(2):
                load_gu(gi)
            for gi in range(RD):
                load_wd(gi)

            NGI = 72
            for gi in range(NGI):
                grp, icg = gi // 8, gi % 8
                sl = gi % 2
                for tt in range(2):
                    w = tt % 2
                    s_gs[w].wait(ten)
                    s_wg[sl].wait(ten)
                    for dc in range(NDC):
                        i = ten.matmul(
                            pg[w][:], wg_sl[sl][:, dc * 128:(dc + 1) * 128],
                            x1b[:, dc * TOK + tt * 512: dc * TOK + tt * 512 + 512],
                            start=(dc == 0), stop=(dc == NDC - 1))
                    (s_wgf[sl] if tt == 1 else s_mmE).inc(i)
                    s_us[w].wait(ten)
                    s_wu[sl].wait(ten)
                    for dc in range(NDC):
                        i = ten.matmul(
                            pu[w][:], wu_sl[sl][:, dc * 128:(dc + 1) * 128],
                            x1b[:, dc * TOK + tt * 512: dc * TOK + tt * 512 + 512],
                            start=(dc == 0), stop=(dc == NDC - 1))
                    (s_wuf[sl] if tt == 1 else s_mmE).inc(i)
                    (s_wgf[sl] if tt == 1 else s_mmE).wait(sca)
                    s_gs[w].wait(sca)
                    s_gs[w].inc(sca.activation(gs[w][:], pg[w][:], AF.Silu))
                    (s_wuf[sl] if tt == 1 else s_mmE).wait(vec)
                    s_us[w].wait(vec)
                    s_gs[w].wait(vec)
                    s_us[w].inc(vec.tensor_copy(us[w][:], pu[w][:]))
                    s_gs[w].wait(vec)
                    s_us[w].wait(vec)
                    if grp > 0:
                        s_wdf[(gi - 8) % RD].wait(vec)
                    s_gs[w].inc(vec.tensor_tensor(
                        h_buf[:, icg * TOK + tt * 512: icg * TOK + tt * 512 + 512],
                        gs[w][:], us[w][:], ALU.mult))
                if gi + 2 < NGI:
                    load_gu(gi + 2)
                if icg == 7:
                    routed = grp == 8
                    s_gs[0].wait(ten)
                    s_gs[1].wait(ten)
                    for oc in range(NDC):
                        for tt in range(2):
                            w = tt % 2
                            s_dn[w].wait(ten)
                            if oc == 0:
                                for ic2 in range(8):
                                    s_wd[(grp * 8 + ic2) % RD].wait(ten)
                            for ic2 in range(8):
                                gi2 = grp * 8 + ic2
                                dsl = gi2 % RD
                                i = ten.matmul(
                                    pdn[w][:], wd_sl[dsl][:, oc * 128:(oc + 1) * 128],
                                    h_buf[:, ic2 * TOK + tt * 512: ic2 * TOK + tt * 512 + 512],
                                    start=(ic2 == 0), stop=(ic2 == 7))
                                if oc == NDC - 1 and tt == 1:
                                    s_wdf[dsl].inc(i)
                            if not (oc == NDC - 1 and tt == 1):
                                s_mmE.inc(i)
                            cover = s_wdf[(grp * 8 + 7) % RD] if (
                                oc == NDC - 1 and tt == 1) else s_mmE
                            cover.wait(vec)
                            xsl = xsum[:, oc * TOK + tt * 512: oc * TOK + tt * 512 + 512]
                            if routed:
                                s_acc.wait(vec)
                                s_cbc.wait(vec)
                                s_dn[w].inc(vec.tensor_tensor(
                                    tmp_r[:], pdn[w][:],
                                    comb_bc[:, tt * 512:(tt + 1) * 512],
                                    ALU.mult))
                                s_dn[w].wait(vec)
                                s_acc.inc(vec.tensor_tensor(xsl, xsl, tmp_r[:],
                                                            ALU.add))
                            else:
                                s_dn[w].inc(vec.tensor_tensor(xsl, xsl,
                                                              pdn[w][:],
                                                              ALU.add))
                    for ic2 in range(8):
                        if grp * 8 + 8 + ic2 < NGI:
                            load_wd(grp * 8 + 8 + ic2)

            stM.close()
            s_acc.wait(vec)
            s_dn[0].wait(vec)
            s_dn[1].wait(vec)
            vec_mark(s_ln)
            emit_ln("2", False)
            s_ln.wait(syn)
            i = syn.dma_start(out_.rearrange("(c p) t -> p c t", p=128),
                              xsum[:].rearrange("p (c t) -> p c t", c=NDC))
            s_out.inc(i, 16)
            s_out.wait(syn)
    return nc


def _deint(nh):
    p = np.concatenate([np.arange(0, 64, 2), np.arange(1, 64, 2)])
    return np.concatenate([h * 64 + p for h in range(nh)])


def _qrope_perm():
    idx = []
    for g4 in range(4):
        for h in range(4 * g4, 4 * g4 + 4):
            idx.append(h * 64 + np.arange(0, 64, 2))
        for h in range(4 * g4, 4 * g4 + 4):
            idx.append(h * 64 + np.arange(1, 64, 2))
    return np.concatenate(idx)


def _wku_perm():
    idx = []
    for g in range(NH // 2):
        h0, h1 = 2 * g, 2 * g + 1
        idx += [np.arange(h0 * 256, h0 * 256 + 128),
                np.arange(h1 * 256, h1 * 256 + 128),
                np.arange(h0 * 256 + 128, h0 * 256 + 256),
                np.arange(h1 * 256 + 128, h1 * 256 + 256)]
    return np.concatenate(idx)


def kernel(**inp):
    x = np.ascontiguousarray(inp["L0_attn_x"], dtype=np.float32)
    T = lambda a: np.ascontiguousarray(np.asarray(a, dtype=np.float32).T)
    shared = {
        "wqd": T(inp["L0_attn_wq_down"]),
        "wqu": T(inp["L0_attn_wq_up"]),
        "wqr": np.ascontiguousarray(T(inp["L0_attn_wq_rope"])[:, _qrope_perm()]),
        "qnw": np.ascontiguousarray(
            np.asarray(inp["L0_attn_q_norm_w"], np.float32).reshape(NRC, 128).T),
        "wkd": T(inp["L0_attn_wkv_down"]),
        "knw": np.ascontiguousarray(
            np.asarray(inp["L0_attn_kv_norm_w"], np.float32).reshape(NKC, 128).T),
        "wku": np.ascontiguousarray(T(inp["L0_attn_wkv_up"])[:, _wku_perm()]),
        "wkr": np.ascontiguousarray(T(inp["L0_attn_wk_rope"])[:, _deint(1)]),
        "wo": T(inp["L0_attn_wo"]),
        "rw": T(inp["L0_ffn_router_weight"]),
        "sg": T(inp["L0_ffn_sh_gate"]),
        "su": T(inp["L0_ffn_sh_up"]),
        "sd": T(inp["L0_ffn_sh_down"]),
        "rg": T(inp["L0_ffn_r_gate"]),
        "ru": T(inp["L0_ffn_r_up"]),
        "rd": T(inp["L0_ffn_r_down"]),
    }
    in_maps = []
    cols_l = []
    for core in range(8):
        b, j = core // 4, core % 4
        chunks = [4 * (t - 1) + j for t in range(1, 9)]
        cols = np.concatenate([np.arange(c * 128, c * 128 + 128) for c in chunks])
        cols_l.append((b, cols))
        xT = x[b].T
        m = dict(shared)
        m["xo"] = np.ascontiguousarray(xT[:, cols])
        m["xf"] = np.ascontiguousarray(xT)
        m["msk"] = np.where(
            np.arange(512)[None, :] <= 128 * j + np.arange(128)[:, None],
            np.float32(0), np.float32(-1e30)).astype(np.float32)
        m["pos"] = cols.astype(np.float32).reshape(1, TOK)
        in_maps.append(m)

    nc = build_nc()
    res = run_bass_kernel_spmd(nc, in_maps, core_ids=list(range(8)))
    full = np.zeros((2, S, D), dtype=np.float32)
    for core in range(8):
        b, cols = cols_l[core]
        full[b, cols, :] = np.asarray(res.results[core]["out"]).T
    return full
